# version 21 (final)
# baseline (speedup 1.0000x reference)
import numpy as np
from contextlib import ExitStack
import concourse.bass as bass
import concourse.mybir as mybir
from concourse.bass_utils import run_bass_kernel_spmd

F32 = mybir.dt.float32
BF16 = mybir.dt.bfloat16
I32 = mybir.dt.int32
AF = mybir.ActivationFunctionType
ALU = mybir.AluOpType

T = 4096
C = 1024
NT = C // 128
W = 512
NCH = T // W
EPS = 1e-5

FUSED = True



def ln_stats_chunk_tensor(tensor, ones_bf, xb, xq, ps_sum, ps_sq, w=W):
    for i in range(NT):
        tensor.matmul(out=ps_sum[:, 0:w], lhsT=ones_bf[:, 0:128], rhs=xb[:, i * w:(i + 1) * w],
                      start=(i == 0), stop=(i == NT - 1))
    m = None
    for i in range(NT):
        m = tensor.matmul(out=ps_sq[:, 0:w], lhsT=ones_bf[:, 0:128], rhs=xq[:, i * w:(i + 1) * w],
                          start=(i == 0), stop=(i == NT - 1))
    return m



def p_l0ffn(nc, semf, src, dst, g_ap, b_ap, tag):
    es = ExitStack()
    sem = lambda n: semf(n)
    with es:
        sb = lambda n, s, d: es.enter_context(nc.sbuf_tensor(tag + n, s, d))
        zt = sb("zt", [128, 2 * NT * W], F32)
        xb = sb("xb", [128, 2 * NT * W], BF16)
        xq = sb("xq", [128, 2 * NT * W], BF16)
        u = sb("u", [128, NT * W], F32)
        mu = sb("mu", [128, W], F32)
        t1 = sb("t1", [128, W], F32)
        t2 = sb("t2", [128, W], F32)
        t3 = sb("t3", [128, W], F32)
        ss = sb("ss", [128, W], F32)
        gcol = sb("gcol", [128, NT], F32)
        bcol = sb("bcol", [128, NT], F32)
        gh = sb("gh", [128, NT], F32)
        bh = sb("bh", [128, NT], F32)
        epsv = sb("epsv", [128, 1], F32)
        eps2v = sb("eps2v", [128, 1], F32)
        ones = sb("ones", [128, 128], BF16)
        ps_sum = es.enter_context(nc.psum_tensor(tag + "pss", [128, W], F32))
        ps_sq = es.enter_context(nc.psum_tensor(tag + "psq", [128, W], F32))

        ldp = (sem("ld0"), sem("ld1"))
        pre = sem("pre")
        cast = sem("cast")
        sq = sem("sq")
        mmst = sem("mmst")
        sA = sem("sA")
        vA = sem("vA")
        sB = sem("sB")
        vB = sem("vB")
        sC = sem("sC")
        vD = sem("vD")
        fin = sem("fin")
        st = sem("st")

        def BLK(c, i):
            return slice((c % 2) * NT * W + i * W, (c % 2) * NT * W + (i + 1) * W)

        with nc.Block(tag) as blk:
            @blk.sync
            def _(sync):
                sync.dma_start(out=gcol[:], in_=g_ap).then_inc(pre, 16)
                sync.dma_start(out=bcol[:], in_=b_ap).then_inc(pre, 16)
                src3 = src.rearrange("(i p) t -> p i t", p=128)
                dst3 = dst.rearrange("(i p) t -> p i t", p=128)
                for c in range(NCH):
                    if c >= 2:
                        sync.wait_ge(st, 16 * (c - 1))
                    sync.dma_start(out=zt[:, (c % 2) * NT * W:(c % 2 + 1) * NT * W].rearrange(
                                       "p (i t) -> p i t", i=NT),
                                   in_=src3[:, :, c * W:(c + 1) * W]).then_inc(ldp[c % 2], 16)
                    sync.wait_ge(fin, NT * (c + 1))
                    sync.dma_start(out=dst3[:, :, c * W:(c + 1) * W],
                                   in_=zt[:, (c % 2) * NT * W:(c % 2 + 1) * NT * W].rearrange(
                                       "p (i t) -> p i t", i=NT)).then_inc(st, 16)
                sync.wait_ge(st, 16 * NCH)

            @blk.scalar
            def _(scalar):
                for c in range(NCH):
                    scalar.wait_ge(ldp[c % 2], 16 * (c // 2 + 1))
                    if c >= 2:
                        scalar.wait_ge(mmst, c - 1)
                    for i in range(NT):
                        ins = scalar.copy(out=xb[:, BLK(c, i)], in_=zt[:, BLK(c, i)])
                    ins.then_inc(cast, 1)
                    scalar.wait_ge(mmst, c + 1)
                    scalar.activation(out=mu[:], in_=ps_sum[:, 0:W], func=AF.Copy, scale=1.0 / C)
                    scalar.activation(out=t1[:], in_=ps_sq[:, 0:W], func=AF.Copy,
                                      scale=1.0 / C).then_inc(sA, 1)
                    scalar.wait_ge(vA, c + 1)
                    scalar.activation(out=t2[:], in_=t1[:], func=AF.Ln, bias=eps2v[:, 0:1],
                                      scale=float(1.0 + EPS))
                    scalar.activation(out=ss[:], in_=t2[:], func=AF.Exp, scale=-0.5).then_inc(sC, 1)

            @blk.vector
            def _(vector):
                vector.memset(ones[:], 1.0)
                vector.memset(epsv[:], float(EPS))
                vector.memset(eps2v[:], float(EPS * EPS))
                vector.wait_ge(pre, 32)
                vector.tensor_scalar(out=gh[:], in0=gcol[:], scalar1=0.5, scalar2=None, op0=ALU.mult)
                vector.tensor_scalar(out=bh[:], in0=bcol[:], scalar1=0.5, scalar2=None,
                                     op0=ALU.mult).then_inc(pre, 1)
                for c in range(NCH):
                    vector.wait_ge(cast, c + 1)
                    for i in range(NT):
                        ins = vector.tensor_tensor(out=xq[:, BLK(c, i)], in0=xb[:, BLK(c, i)],
                                                   in1=xb[:, BLK(c, i)], op=ALU.mult)
                    ins.then_inc(sq, 1)
                    vector.wait_ge(sA, c + 1)
                    vector.tensor_tensor(out=t2[:], in0=mu[:], in1=mu[:], op=ALU.mult)
                    vector.tensor_tensor(out=t1[:], in0=t1[:], in1=t2[:],
                                         op=ALU.subtract).then_inc(vA, 1)
                    vector.wait_ge(sC, c + 1)
                    for i in range(NT):
                        sl = BLK(c, i)
                        usl = slice(i * W, (i + 1) * W)
                        vector.tensor_tensor(out=u[:, usl], in0=zt[:, sl], in1=mu[:], op=ALU.subtract)
                        vector.tensor_tensor(out=u[:, usl], in0=u[:, usl], in1=ss[:], op=ALU.mult)
                        vector.scalar_tensor_tensor(out=zt[:, sl], in0=u[:, usl],
                                                    scalar=gh[:, i:i + 1], in1=zt[:, sl],
                                                    op0=ALU.mult, op1=ALU.add)
                        vector.tensor_scalar(out=zt[:, sl], in0=zt[:, sl], scalar1=bh[:, i:i + 1],
                                             scalar2=None, op0=ALU.add).then_inc(fin, 1)

            @blk.gpsimd
            def _(gpsimd):
                pass

            @blk.tensor
            def _(tensor):
                tensor.wait_ge(pre, 33)
                for c in range(NCH):
                    tensor.wait_ge(cast, c + 1)
                    tensor.wait_ge(sq, c + 1)
                    if c > 0:
                        tensor.wait_ge(sA, c)
                    for i in range(NT):
                        tensor.matmul(out=ps_sum[:, 0:W], lhsT=ones[:, 0:128], rhs=xb[:, BLK(c, i)],
                                      start=(i == 0), stop=(i == NT - 1))
                    for i in range(NT):
                        m = tensor.matmul(out=ps_sq[:, 0:W], lhsT=ones[:, 0:128], rhs=xq[:, BLK(c, i)],
                                          start=(i == 0), stop=(i == NT - 1))
                    m.then_inc(mmst, 1)



HID = 3072
NH = HID // 128

def p_l1ffn(nc, semf, src, dst, wg_ap, wu_ap, wd_ap, wd_scr, wu_scr, tag):
    es = ExitStack()
    sem = lambda n: semf(n)
    with es:
        sb = lambda n, s, d: es.enter_context(nc.sbuf_tensor(tag + n, s, d))
        zt = sb("zt", [128, 2 * NT * W], F32)
        xb = sb("xb", [128, 2 * NT * W], BF16)
        xq = sb("xq", [128, 2 * NT * W], BF16)
        hbuf = sb("hbuf", [128, NH * W], BF16)
        sg = sb("sg", [128, W], BF16)
        mu = sb("mu", [128, W], F32)
        t1 = sb("t1", [128, W], F32)
        t2 = sb("t2", [128, W], F32)
        t3 = sb("t3", [128, W], F32)
        epsv = sb("epsv", [128, 1], F32)
        ones = sb("ones", [128, 128], BF16)
        wgi = sb("wgi", [128, HID], I32)
        wdq = sb("wdq", [128, C], BF16)
        wg_sb = sb("wg_sb", [128, NT * HID], BF16)
        wuq = sb("wuq", [128, HID], BF16)
        wustr = sb("wustr", [128, 4 * NT * 128], BF16)
        wdst = sb("wdst", [128, 4 * NH * 128], BF16)
        ps_sum = es.enter_context(nc.psum_tensor(tag + "pss", [128, W], F32))
        ps_sq = es.enter_context(nc.psum_tensor(tag + "psq", [128, W], F32))
        ps_g = es.enter_context(nc.psum_tensor(tag + "psg", [128, 2 * W], F32))
        ps_u = es.enter_context(nc.psum_tensor(tag + "psu", [128, 2 * W], F32))
        ps_d = es.enter_context(nc.psum_tensor(tag + "psd", [128, 2 * W], F32))

        ldp = (sem("ld0"), sem("ld1"))
        wldp = (sem("wld0"), sem("wld1"))
        pre = sem("pre")
        wdsp = (sem("wds0"), sem("wds1"))
        wdlp = (sem("wdl0"), sem("wdl1"), sem("wdl2"), sem("wdl3"))
        wusp = (sem("wus0"), sem("wus1"))
        wulp = (sem("wul0"), sem("wul1"), sem("wul2"), sem("wul3"))
        cast = sem("cast")
        sq = sem("sq")
        mmst = sem("mmst")
        sA = sem("sA")
        vA = sem("vA")
        sB = sem("sB")
        xn_s = sem("xn")
        mmg = sem("mmg")
        mmu = sem("mmu")
        mmd = sem("mmd")
        sil = sem("sil")
        hbv = sem("hbv")
        fin = sem("fin")
        st = sem("st")

        def PAR(c):
            return c % 2

        def S(c, base_w=W):
            return slice(0, base_w)

        def BLK(c, i, w=W):
            off = PAR(c) * NT * w
            return slice(off + i * w, off + (i + 1) * w)

        with nc.Block(tag) as blk:
            @blk.sync
            def _(sync):
                for kk in range(16):
                    if kk >= 1:
                        sync.wait_ge(pre, kk)
                    ap = wg_ap if kk < 8 else wu_ap
                    i = kk % 8
                    sync.dma_start(out=wgi[:, 0:HID],
                                   in_=ap[i * 128:(i + 1) * 128, :]).then_inc(wldp[kk % 2], 16)
                    if kk >= 8:
                        sync.wait_ge(pre, kk + 1)
                        sync.dma_start(out=wu_scr[i * 128:(i + 1) * 128, :],
                                       in_=wuq[:, 0:HID]).then_inc(wusp[i % 2], 16)
                for j in range(NH):
                    kk = 16 + j
                    sync.wait_ge(pre, kk)
                    sync.dma_start(out=wgi[:, 0:C],
                                   in_=wd_ap[j * 128:(j + 1) * 128, :]).then_inc(wldp[kk % 2], 16)
                    sync.wait_ge(pre, kk + 1)
                    sync.dma_start(out=wd_scr[j * 128:(j + 1) * 128, :],
                                   in_=wdq[:, 0:C]).then_inc(wdsp[j % 2], 16)
                src3 = src.rearrange("(i p) t -> p i t", p=128)
                dst3 = dst.rearrange("(i p) t -> p i t", p=128)
                wd3 = wd_scr.rearrange("(kh p) c -> p kh c", p=128)
                wu3 = wu_scr.rearrange("(ct p) h -> p ct h", p=128)
                def emit_load(cc):
                    if cc >= 2:
                        sync.wait_ge(st, 16 * (cc - 1))
                    sync.dma_start(out=zt[:, PAR(cc) * NT * W:(PAR(cc) + 1) * NT * W].rearrange(
                                       "p (i t) -> p i t", i=NT),
                                   in_=src3[:, :, cc * W:(cc + 1) * W]).then_inc(ldp[cc % 2], 16)

                emit_load(0)
                if NCH > 1:
                    emit_load(1)
                for c in range(NCH):
                    for h in range(NH):
                        gp = c * NH + h
                        if gp >= 4:
                            sync.wait_ge(mmu, gp - 3)
                        if gp == 0:
                            sync.wait_ge(wusp[0], 16 * 4)
                            sync.wait_ge(wusp[1], 16 * 4)
                        sync.dma_start(
                            out=wustr[:, (gp % 4) * NT * 128:((gp % 4) + 1) * NT * 128].rearrange(
                                "p (ct c) -> p ct c", ct=NT),
                            in_=wu3[:, :, h * 128:(h + 1) * 128]).then_inc(wulp[gp % 4], 16)
                    for d in range(NT):
                        g = c * NT + d
                        if g >= 4:
                            sync.wait_ge(mmd, g - 3)
                        if g == 0:
                            sync.wait_ge(wdsp[0], 16 * (NH // 2))
                            sync.wait_ge(wdsp[1], 16 * (NH // 2))
                        sync.dma_start(
                            out=wdst[:, (g % 4) * NH * 128:((g % 4) + 1) * NH * 128].rearrange(
                                "p (kh c) -> p kh c", kh=NH),
                            in_=wd3[:, :, d * 128:(d + 1) * 128]).then_inc(wdlp[g % 4], 16)
                    sync.wait_ge(fin, NT * (c + 1))
                    sync.dma_start(out=dst3[:, :, c * W:(c + 1) * W],
                                   in_=zt[:, PAR(c) * NT * W:(PAR(c) + 1) * NT * W].rearrange(
                                       "p (i t) -> p i t", i=NT)).then_inc(st, 16)
                    if c + 2 < NCH:
                        emit_load(c + 2)
                sync.wait_ge(st, 16 * NCH)

            @blk.gpsimd
            def _(gpsimd):
                pass

            @blk.vector
            def _(vector):
                vector.memset(ones[:], 1.0)
                vector.memset(epsv[:], float(EPS))
                def emit_prologue(cc):
                    vector.wait_ge(cast, cc + 1)
                    for i in range(NT):
                        ins = vector.tensor_tensor(out=xq[:, BLK(cc, i)], in0=xb[:, BLK(cc, i)],
                                                   in1=xb[:, BLK(cc, i)], op=ALU.mult)
                    ins.then_inc(sq, 1)
                    vector.wait_ge(sA, cc + 1)
                    vector.tensor_tensor(out=t2[:, S(cc)], in0=mu[:, S(cc)], in1=mu[:, S(cc)],
                                         op=ALU.mult)
                    vector.tensor_tensor(out=t1[:, S(cc)], in0=t1[:, S(cc)], in1=t2[:, S(cc)],
                                         op=ALU.subtract).then_inc(vA, 1)
                    vector.wait_ge(sB, cc + 1)
                    vector.wait_ge(mmst, cc + 1)
                    for i in range(NT):
                        vector.tensor_tensor(out=t2[:, S(cc)], in0=zt[:, BLK(cc, i)], in1=mu[:, S(cc)],
                                             op=ALU.subtract)
                        ins = vector.tensor_tensor(out=xq[:, BLK(cc, i)], in0=t2[:, S(cc)],
                                                   in1=t3[:, S(cc)], op=ALU.mult)
                    ins.then_inc(xn_s, 1)

                emit_prologue(0)
                for c in range(NCH):
                    for h in range(NH):
                        vector.wait_ge(sil, c * NH + h + 1)
                        vector.wait_ge(mmu, c * NH + h + 1)
                        vector.tensor_tensor(out=hbuf[:, h * W:(h + 1) * W],
                                             in0=ps_u[:, (h % 2) * W:(h % 2 + 1) * W],
                                             in1=sg[:, 0:W],
                                             op=ALU.mult).then_inc(hbv, 1)
                    if c + 1 < NCH:
                        emit_prologue(c + 1)
                    for d in range(NT):
                        vector.wait_ge(mmd, c * NT + d + 1)
                        vector.scalar_tensor_tensor(out=zt[:, BLK(c, d)],
                                                    in0=ps_d[:, (d % 2) * W:(d % 2 + 1) * W],
                                                    scalar=0.5, in1=zt[:, BLK(c, d)],
                                                    op0=ALU.mult, op1=ALU.add).then_inc(fin, 1)

            @blk.scalar
            def _(scalar):
                for kk in range(16):
                    i = kk % 8
                    scalar.wait_ge(wldp[kk % 2], 16 * (kk // 2 + 1))
                    if kk < 8:
                        scalar.activation(out=wg_sb[:, i * HID:(i + 1) * HID], in_=wgi[:, 0:HID],
                                          func=AF.Copy, scale=1.0 / 127.0).then_inc(pre, 1)
                    else:
                        if kk >= 9:
                            scalar.wait_ge(wusp[(i - 1) % 2], 16 * ((i - 1) // 2 + 1))
                        scalar.activation(out=wuq[:, 0:HID], in_=wgi[:, 0:HID],
                                          func=AF.Copy, scale=1.0 / 127.0).then_inc(pre, 1)
                for j in range(NH):
                    kk = 16 + j
                    scalar.wait_ge(wldp[kk % 2], 16 * (kk // 2 + 1))
                    if j >= 1:
                        scalar.wait_ge(wdsp[(j - 1) % 2], 16 * ((j - 1) // 2 + 1))
                    scalar.activation(out=wdq[:, 0:C], in_=wgi[:, 0:C],
                                      func=AF.Copy, scale=1.0 / 127.0).then_inc(pre, 1)
                for c in range(NCH):
                    scalar.wait_ge(ldp[c % 2], 16 * (c // 2 + 1))
                    if c >= 2:
                        scalar.wait_ge(mmst, c - 1)
                    for i in range(NT):
                        ins = scalar.copy(out=xb[:, BLK(c, i)], in_=zt[:, BLK(c, i)])
                    ins.then_inc(cast, 1)
                    scalar.wait_ge(mmst, c + 1)
                    scalar.activation(out=mu[:, S(c)], in_=ps_sum[:, 0:W], func=AF.Copy, scale=1.0 / C)
                    scalar.activation(out=t1[:, S(c)], in_=ps_sq[:, 0:W], func=AF.Copy,
                                      scale=1.0 / C).then_inc(sA, 1)
                    scalar.wait_ge(vA, c + 1)
                    scalar.activation(out=t2[:, S(c)], in_=t1[:, S(c)], func=AF.Ln, bias=epsv[:, 0:1])
                    scalar.activation(out=t3[:, S(c)], in_=t2[:, S(c)], func=AF.Exp,
                                      scale=-0.5).then_inc(sB, 1)
                    for h in range(NH):
                        scalar.wait_ge(mmg, c * NH + h + 1)
                        if c * NH + h >= 1:
                            scalar.wait_ge(hbv, c * NH + h)
                        scalar.activation(out=sg[:, 0:W],
                                          in_=ps_g[:, (h % 2) * W:(h % 2 + 1) * W],
                                          func=AF.Silu).then_inc(sil, 1)

            @blk.tensor
            def _(tensor):
                tensor.wait_ge(pre, 16)

                def emit_stats(cc):
                    tensor.wait_ge(cast, cc + 1)
                    tensor.wait_ge(sq, cc + 1)
                    if cc > 0:
                        tensor.wait_ge(sA, cc)
                    for i in range(NT):
                        tensor.matmul(out=ps_sum[:, 0:W], lhsT=ones[:, 0:128], rhs=xb[:, BLK(cc, i)],
                                      start=(i == 0), stop=(i == NT - 1))
                    for i in range(NT):
                        m = tensor.matmul(out=ps_sq[:, 0:W], lhsT=ones[:, 0:128], rhs=xq[:, BLK(cc, i)],
                                          start=(i == 0), stop=(i == NT - 1))
                    m.then_inc(mmst, 1)

                emit_stats(0)
                for c in range(NCH):
                    tensor.wait_ge(xn_s, c + 1)
                    for h in range(NH):
                        if c * NH + h >= 2:
                            tensor.wait_ge(sil, c * NH + h - 1)
                            tensor.wait_ge(hbv, c * NH + h - 1)
                        for ct in range(NT):
                            m = tensor.matmul(out=ps_g[:, (h % 2) * W:(h % 2 + 1) * W],
                                              lhsT=wg_sb[:, ct * HID + h * 128: ct * HID + (h + 1) * 128],
                                              rhs=xq[:, BLK(c, ct)],
                                              start=(ct == 0), stop=(ct == NT - 1))
                        m.then_inc(mmg, 1)
                        gp = c * NH + h
                        tensor.wait_ge(wulp[gp % 4], 16 * (gp // 4 + 1))
                        for ct in range(NT):
                            m = tensor.matmul(out=ps_u[:, (h % 2) * W:(h % 2 + 1) * W],
                                              lhsT=wustr[:, ((gp % 4) * NT + ct) * 128:((gp % 4) * NT + ct + 1) * 128],
                                              rhs=xq[:, BLK(c, ct)],
                                              start=(ct == 0), stop=(ct == NT - 1))
                        m.then_inc(mmu, 1)
                    if c + 1 < NCH:
                        emit_stats(c + 1)
                    for d in range(NT):
                        g = c * NT + d
                        if g >= 2:
                            tensor.wait_ge(fin, g - 1)
                        tensor.wait_ge(wdlp[g % 4], 16 * (g // 4 + 1))
                        for kh in range(NH):
                            tensor.wait_ge(hbv, c * NH + kh + 1)
                            m = tensor.matmul(out=ps_d[:, (d % 2) * W:(d % 2 + 1) * W],
                                              lhsT=wdst[:, ((g % 4) * NH + kh) * 128:((g % 4) * NH + kh + 1) * 128],
                                              rhs=hbuf[:, kh * W:(kh + 1) * W],
                                              start=(kh == 0), stop=(kh == NH - 1))
                        m.then_inc(mmd, 1)



WR = 256
NCHR = T // WR

def p_rwkv_att(nc, semf, src, dst, prm, wo_scr, tag):
    WA = 512
    NCA = T // WA
    es = ExitStack()
    sem = lambda n: semf(n)
    with es:
        sb = lambda n, s, d: es.enter_context(nc.sbuf_tensor(tag + n, s, d))
        zt = sb("zt", [128, 2 * NT * WA], F32)
        xb = sb("xb", [128, NT * WA], BF16)
        xq = sb("xq", [128, NT * WA], BF16)
        wmx = sb("wmx", [128, NT * (WA + 1)], BF16)
        Et = sb("Et", [128, NT * WA], F32)
        Gt = sb("Gt", [128, NT * WA], F32)
        At = sb("At", [128, NT * WA], F32)
        Bt = sb("Bt", [128, NT * WA], F32)
        srt = sb("srt", [128, NT * WA], BF16)
        rwb = sb("rwb", [128, NT * WA], BF16)
        mu = sb("mu", [128, WA], F32)
        t1 = sb("t1", [128, WA], F32)
        t2 = sb("t2", [128, WA], F32)
        t3 = sb("t3", [128, WA], F32)
        ss = sb("ss", [128, WA], F32)
        epsv = sb("epsv", [128, 1], F32)
        eps2v = sb("eps2v", [128, 1], F32)
        ones = sb("ones", [128, 128], BF16)
        prmt = sb("prmt", [128, 7 * NT], F32)
        eu = sb("eu", [128, NT], F32)
        lcol = sb("lcol", [128, NT], F32)
        omk = sb("omk", [128, 3 * NT], F32)
        cA = sb("cA", [128, NT], F32)
        cB = sb("cB", [128, NT], F32)
        wstg = sb("wstg", [128, C], F32)
        wk_sb = sb("wk_sb", [128, NT * C], BF16)
        wv_sb = sb("wv_sb", [128, NT * C], BF16)
        wr_sb = sb("wr_sb", [128, NT * C], BF16)
        wob = sb("wob", [128, C], BF16)
        wost = sb("wost", [128, 2 * NT * 128], BF16)
        ps_sum = es.enter_context(nc.psum_tensor(tag + "pss", [128, WA], F32))
        ps_sq = es.enter_context(nc.psum_tensor(tag + "psq", [128, WA], F32))
        ps_p0 = es.enter_context(nc.psum_tensor(tag + "pp0", [128, WA], F32))
        ps_p1 = es.enter_context(nc.psum_tensor(tag + "pp1", [128, WA], F32))
        ps_o0 = es.enter_context(nc.psum_tensor(tag + "po0", [128, WA], F32))
        ps_o1 = es.enter_context(nc.psum_tensor(tag + "po1", [128, WA], F32))
        ps_p = (ps_p0, ps_p1)
        ps_o = (ps_o0, ps_o1)

        ldp = (sem("ld0"), sem("ld1")); prs = sem("prs"); wcv = sem("wcv")
        wldp = (sem("wld0"), sem("wld1"))
        wosp = (sem("wos0"), sem("wos1"))
        wolp = (sem("wol0"), sem("wol1"))
        sprep = sem("sprep"); vprep = sem("vprep")
        cast = sem("cast"); sq = sem("sq"); mmst = sem("mmst")
        sA = sem("sA"); vA = sem("vA"); sB = sem("sB"); vB = sem("vB"); sC = sem("sC")
        xkr = sem("xkr"); xvr = sem("xvr"); xrr = sem("xrr")
        mmk = sem("mmk"); mmv = sem("mmv"); mmr = sem("mmr"); mmo = sem("mmo")
        ek = sem("ek"); gv = sem("gv"); rk = sem("rk"); rv = sem("rv")
        fin = sem("fin"); st = sem("st")

        def ZB(c, i):
            off = (c % 2) * NT * WA
            return slice(off + i * WA, off + (i + 1) * WA)

        w_aps = [prm["wkT"], prm["wvT"], prm["wrT"], prm["woT"]]
        w_sbs = [wk_sb, wv_sb, wr_sb, None]

        with nc.Block(tag) as blk:
            @blk.sync
            def _(sync):
                for j, p in enumerate([prm["ln1g"], prm["ln1b"], prm["mixk"], prm["mixv"],
                                       prm["mixr"], prm["td"], prm["tf"]]):
                    sync.dma_start(out=prmt[:, j * NT:(j + 1) * NT], in_=p).then_inc(prs, 16)
                for kk in range(32):
                    if kk >= 1:
                        sync.wait_ge(wcv, kk)
                    w_ap = w_aps[kk // 8]
                    i = kk % 8
                    sync.dma_start(out=wstg[:, 0:C],
                                   in_=w_ap[i * 128:(i + 1) * 128, :]).then_inc(wldp[kk % 2], 16)
                    if kk >= 24:
                        sync.wait_ge(wcv, kk + 1)
                        sync.dma_start(out=wo_scr[i * 128:(i + 1) * 128, :],
                                       in_=wob[:, 0:C]).then_inc(wosp[i % 2], 16)
                src3 = src.rearrange("(i p) t -> p i t", p=128)
                dst3 = dst.rearrange("(i p) t -> p i t", p=128)
                wo3 = wo_scr.rearrange("(dt p) c -> p dt c", p=128)

                def emit_load(cc):
                    if cc >= 2:
                        sync.wait_ge(st, 16 * (cc - 1))
                    sync.dma_start(out=zt[:, (cc % 2) * NT * WA:(cc % 2 + 1) * NT * WA].rearrange(
                                       "p (i t) -> p i t", i=NT),
                                   in_=src3[:, :, cc * WA:(cc + 1) * WA]).then_inc(ldp[cc % 2], 16)

                emit_load(0)
                if NCA > 1:
                    emit_load(1)
                for c in range(NCA):
                    for jo in range(NT):
                        g = c * NT + jo
                        if g >= 2:
                            sync.wait_ge(mmo, g - 1)
                        if g == 0:
                            sync.wait_ge(wosp[0], 16 * 4)
                            sync.wait_ge(wosp[1], 16 * 4)
                        sync.dma_start(
                            out=wost[:, (g % 2) * NT * 128:((g % 2) + 1) * NT * 128].rearrange(
                                "p (dt c) -> p dt c", dt=NT),
                            in_=wo3[:, :, jo * 128:(jo + 1) * 128]).then_inc(wolp[g % 2], 16)
                    sync.wait_ge(fin, NT * (c + 1))
                    zt3c = zt[:, (c % 2) * NT * WA:(c % 2 + 1) * NT * WA].rearrange(
                        "p (i t) -> p i t", i=NT)
                    sync.dma_start(out=dst3[:, :, c * WA:(c + 1) * WA], in_=zt3c).then_inc(st, 16)
                    if c + 2 < NCA:
                        emit_load(c + 2)
                sync.wait_ge(st, 16 * NCA)

            @blk.scalar
            def _(scalar):
                scalar.wait_ge(prs, 112)
                scalar.activation(out=eu[:], in_=prmt[:, 6 * NT:7 * NT], func=AF.Exp)
                scalar.activation(out=lcol[:], in_=prmt[:, 5 * NT:6 * NT], func=AF.Exp)
                scalar.activation(out=lcol[:], in_=lcol[:], func=AF.Exp,
                                  scale=-1.0).then_inc(sprep, 1)
                for kk in range(32):
                    scalar.wait_ge(wldp[kk % 2], 16 * (kk // 2 + 1))
                    i = kk % 8
                    if kk < 24:
                        scalar.copy(out=w_sbs[kk // 8][:, i * C:(i + 1) * C],
                                    in_=wstg[:, 0:C]).then_inc(wcv, 1)
                    else:
                        if kk >= 25:
                            scalar.wait_ge(wosp[(i - 1) % 2], 16 * ((i - 1) // 2 + 1))
                        scalar.copy(out=wob[:, 0:C], in_=wstg[:, 0:C]).then_inc(wcv, 1)
                for c in range(NCA):
                    scalar.wait_ge(ldp[c % 2], 16 * (c // 2 + 1))
                    for i in range(NT):
                        ins = scalar.copy(out=xb[:, i * WA:(i + 1) * WA], in_=zt[:, ZB(c, i)])
                    ins.then_inc(cast, 1)
                    if c > 0:
                        scalar.wait_ge(mmr, NT * c)
                    for i in range(NT):
                        ins = scalar.activation(out=xq[:, i * WA:(i + 1) * WA],
                                                in_=xb[:, i * WA:(i + 1) * WA], func=AF.Square)
                    ins.then_inc(sq, 1)
                    scalar.wait_ge(mmst, c + 1)
                    scalar.activation(out=mu[:], in_=ps_sum[:, 0:WA], func=AF.Copy, scale=1.0 / C)
                    scalar.activation(out=t1[:], in_=ps_sq[:, 0:WA], func=AF.Copy,
                                      scale=1.0 / C).then_inc(sA, 1)
                    scalar.wait_ge(vA, c + 1)
                    scalar.activation(out=t2[:], in_=t1[:], func=AF.Ln, bias=epsv[:, 0:1])
                    scalar.activation(out=t3[:], in_=t2[:], func=AF.Exp, scale=-0.5)
                    scalar.activation(out=t2[:], in_=t1[:], func=AF.Ln, bias=eps2v[:, 0:1],
                                      scale=float(1.0 + EPS))
                    scalar.activation(out=ss[:], in_=t2[:], func=AF.Exp, scale=-0.5).then_inc(sC, 1)
                    for i in range(NT):
                        scalar.wait_ge(mmk, NT * c + i + 1)
                        scalar.activation(out=Et[:, i * WA:(i + 1) * WA],
                                          in_=ps_p[i % 2][:, 0:WA], func=AF.Exp).then_inc(ek, 1)
                    for i in range(NT):
                        scalar.wait_ge(mmr, NT * c + i + 1)
                        scalar.activation(out=srt[:, i * WA:(i + 1) * WA],
                                          in_=ps_p[i % 2][:, 0:WA], func=AF.Sigmoid).then_inc(rk, 1)

            @blk.vector
            def _(vector):
                mixbuf = xq
                vector.memset(ones[:], 1.0)
                vector.memset(epsv[:], float(EPS))
                vector.memset(eps2v[:], float(EPS * EPS))
                vector.memset(cA[:], 0.0)
                vector.memset(cB[:], 0.0)
                for i in range(NT):
                    vector.memset(wmx[:, i * (WA + 1):i * (WA + 1) + 1], 0.0)
                vector.wait_ge(prs, 112)
                for j in range(3):
                    vector.tensor_scalar(out=omk[:, j * NT:(j + 1) * NT],
                                         in0=prmt[:, (2 + j) * NT:(3 + j) * NT],
                                         scalar1=-1.0, scalar2=1.0, op0=ALU.mult, op1=ALU.add)
                vector.wait_ge(sprep, 1)
                vector.engine_nop().then_inc(vprep, NT)
                gcol = prmt[:, 0:NT]
                bcol = prmt[:, NT:2 * NT]
                def emit_prologue(cc):
                    vector.wait_ge(sA, cc + 1)
                    vector.tensor_tensor(out=t2[:], in0=mu[:], in1=mu[:], op=ALU.mult)
                    vector.tensor_tensor(out=t1[:], in0=t1[:], in1=t2[:],
                                         op=ALU.subtract).then_inc(vA, 1)
                    vector.wait_ge(sC, cc + 1)
                    vector.wait_ge(mmst, cc + 1)
                    for i in range(NT):
                        vector.tensor_tensor(out=t2[:], in0=zt[:, ZB(cc, i)], in1=mu[:],
                                             op=ALU.subtract)
                        vector.tensor_tensor(out=t2[:], in0=t2[:], in1=ss[:], op=ALU.mult)
                        vector.tensor_scalar(out=wmx[:, i * (WA + 1) + 1:(i + 1) * (WA + 1)],
                                             in0=t2[:], scalar1=gcol[:, i:i + 1],
                                             scalar2=bcol[:, i:i + 1], op0=ALU.mult, op1=ALU.add)
                    for i in range(NT):
                        wcur = wmx[:, i * (WA + 1) + 1:(i + 1) * (WA + 1)]
                        wsh = wmx[:, i * (WA + 1):(i + 1) * (WA + 1) - 1]
                        vector.tensor_scalar(out=mixbuf[:, i * WA:(i + 1) * WA], in0=wsh,
                                             scalar1=omk[:, 0 * NT + i:0 * NT + i + 1],
                                             scalar2=None, op0=ALU.mult)
                        ins = vector.scalar_tensor_tensor(out=mixbuf[:, i * WA:(i + 1) * WA],
                                                          in0=wcur,
                                                          scalar=prmt[:, 2 * NT + i:2 * NT + i + 1],
                                                          in1=mixbuf[:, i * WA:(i + 1) * WA],
                                                          op0=ALU.mult, op1=ALU.add)
                    ins.then_inc(xkr, 1)

                for c in range(NCA):
                    emit_prologue(c)
                    vector.wait_ge(mmk, NT * (c + 1))
                    for i in range(NT):
                        wcur = wmx[:, i * (WA + 1) + 1:(i + 1) * (WA + 1)]
                        wsh = wmx[:, i * (WA + 1):(i + 1) * (WA + 1) - 1]
                        vector.tensor_scalar(out=mixbuf[:, i * WA:(i + 1) * WA], in0=wsh,
                                             scalar1=omk[:, 1 * NT + i:1 * NT + i + 1],
                                             scalar2=None, op0=ALU.mult)
                        ins = vector.scalar_tensor_tensor(out=mixbuf[:, i * WA:(i + 1) * WA],
                                                          in0=wcur,
                                                          scalar=prmt[:, 3 * NT + i:3 * NT + i + 1],
                                                          in1=mixbuf[:, i * WA:(i + 1) * WA],
                                                          op0=ALU.mult, op1=ALU.add)
                    ins.then_inc(xvr, 1)
                    for i in range(NT):
                        vector.wait_ge(mmv, NT * c + i + 1)
                        vector.wait_ge(ek, NT * c + i + 1)
                        vector.tensor_tensor(out=Gt[:, i * WA:(i + 1) * WA],
                                             in0=ps_p[i % 2][:, 0:WA],
                                             in1=Et[:, i * WA:(i + 1) * WA], op=ALU.mult).then_inc(gv, 1)
                    for i in range(NT):
                        wcur = wmx[:, i * (WA + 1) + 1:(i + 1) * (WA + 1)]
                        wsh = wmx[:, i * (WA + 1):(i + 1) * (WA + 1) - 1]
                        vector.tensor_scalar(out=mixbuf[:, i * WA:(i + 1) * WA], in0=wsh,
                                             scalar1=omk[:, 2 * NT + i:2 * NT + i + 1],
                                             scalar2=None, op0=ALU.mult)
                        ins = vector.scalar_tensor_tensor(out=mixbuf[:, i * WA:(i + 1) * WA],
                                                          in0=wcur,
                                                          scalar=prmt[:, 4 * NT + i:4 * NT + i + 1],
                                                          in1=mixbuf[:, i * WA:(i + 1) * WA],
                                                          op0=ALU.mult, op1=ALU.add)
                    ins.then_inc(xrr, 1)
                    for i in range(NT):
                        vector.tensor_copy(out=wmx[:, i * (WA + 1):i * (WA + 1) + 1],
                                           in_=wmx[:, (i + 1) * (WA + 1) - 1:(i + 1) * (WA + 1)])
                    for i in range(NT):
                        bsl = slice(i * WA, (i + 1) * WA)
                        lb = lcol[:, i:i + 1].broadcast_to((128, WA))
                        vector.tensor_tensor_scan(out=At[:, bsl], data0=lb, data1=Gt[:, bsl],
                                                  initial=cA[:, i:i + 1], op0=ALU.mult, op1=ALU.add)
                        vector.tensor_tensor_scan(out=Bt[:, bsl], data0=lb, data1=Et[:, bsl],
                                                  initial=cB[:, i:i + 1], op0=ALU.mult, op1=ALU.add)
                        vector.scalar_tensor_tensor(out=Gt[:, i * WA:i * WA + 1],
                                                    in0=Gt[:, i * WA:i * WA + 1],
                                                    scalar=eu[:, i:i + 1], in1=cA[:, i:i + 1],
                                                    op0=ALU.mult, op1=ALU.add)
                        vector.scalar_tensor_tensor(out=Gt[:, i * WA + 1:(i + 1) * WA],
                                                    in0=Gt[:, i * WA + 1:(i + 1) * WA],
                                                    scalar=eu[:, i:i + 1],
                                                    in1=At[:, i * WA:(i + 1) * WA - 1],
                                                    op0=ALU.mult, op1=ALU.add)
                        vector.scalar_tensor_tensor(out=Et[:, i * WA:i * WA + 1],
                                                    in0=Et[:, i * WA:i * WA + 1],
                                                    scalar=eu[:, i:i + 1], in1=cB[:, i:i + 1],
                                                    op0=ALU.mult, op1=ALU.add)
                        vector.scalar_tensor_tensor(out=Et[:, i * WA + 1:(i + 1) * WA],
                                                    in0=Et[:, i * WA + 1:(i + 1) * WA],
                                                    scalar=eu[:, i:i + 1],
                                                    in1=Bt[:, i * WA:(i + 1) * WA - 1],
                                                    op0=ALU.mult, op1=ALU.add)
                        vector.tensor_copy(out=cA[:, i:i + 1], in_=At[:, (i + 1) * WA - 1:(i + 1) * WA])
                        vector.tensor_copy(out=cB[:, i:i + 1], in_=Bt[:, (i + 1) * WA - 1:(i + 1) * WA])
                        vector.reciprocal(out=Bt[:, bsl], in_=Et[:, bsl])
                        vector.tensor_tensor(out=At[:, bsl], in0=Gt[:, bsl], in1=Bt[:, bsl], op=ALU.mult)
                        vector.wait_ge(rk, NT * c + i + 1)
                        ins = vector.tensor_tensor(out=rwb[:, bsl], in0=At[:, bsl], in1=srt[:, bsl],
                                                   op=ALU.mult)
                    ins.then_inc(rv, 1)
                    for j in range(NT):
                        vector.wait_ge(mmo, NT * c + j + 1)
                        sl = ZB(c, j)
                        vector.tensor_tensor(out=t2[:], in0=zt[:, sl], in1=mu[:], op=ALU.subtract)
                        vector.tensor_tensor(out=t2[:], in0=t2[:], in1=t3[:], op=ALU.mult)
                        vector.tensor_tensor(out=zt[:, sl], in0=t2[:], in1=ps_o[j % 2][:, 0:WA],
                                             op=ALU.add).then_inc(fin, 1)

            @blk.tensor
            def _(tensor):
                tensor.wait_ge(wcv, 32)
                tensor.wait_ge(vprep, NT)

                for c in range(NCA):
                    tensor.wait_ge(cast, c + 1)
                    tensor.wait_ge(sq, c + 1)
                    if c > 0:
                        tensor.wait_ge(sA, c)
                    m = ln_stats_chunk_tensor(tensor, ones, xb, xq, ps_sum, ps_sq, w=WA)
                    m.then_inc(mmst, 1)
                    tensor.wait_ge(xkr, c + 1)
                    for i in range(NT):
                        if i >= 2:
                            tensor.wait_ge(ek, NT * c + i - 1)
                        elif c > 0:
                            tensor.wait_ge(rk, NT * c + i - 1)
                        for ct in range(NT):
                            m = tensor.matmul(out=ps_p[i % 2][:, 0:WA],
                                              lhsT=wk_sb[:, ct * C + i * 128:ct * C + (i + 1) * 128],
                                              rhs=xq[:, ct * WA:(ct + 1) * WA],
                                              start=(ct == 0), stop=(ct == NT - 1))
                        m.then_inc(mmk, 1)
                    tensor.wait_ge(xvr, c + 1)
                    for i in range(NT):
                        if i < 2:
                            tensor.wait_ge(ek, NT * c + 7 + i)
                        else:
                            tensor.wait_ge(gv, NT * c + i - 1)
                        for ct in range(NT):
                            m = tensor.matmul(out=ps_p[i % 2][:, 0:WA],
                                              lhsT=wv_sb[:, ct * C + i * 128:ct * C + (i + 1) * 128],
                                              rhs=xq[:, ct * WA:(ct + 1) * WA],
                                              start=(ct == 0), stop=(ct == NT - 1))
                        m.then_inc(mmv, 1)
                    tensor.wait_ge(xrr, c + 1)
                    for i in range(NT):
                        if i < 2:
                            tensor.wait_ge(gv, NT * c + 7 + i)
                        else:
                            tensor.wait_ge(rk, NT * c + i - 1)
                        for ct in range(NT):
                            m = tensor.matmul(out=ps_p[i % 2][:, 0:WA],
                                              lhsT=wr_sb[:, ct * C + i * 128:ct * C + (i + 1) * 128],
                                              rhs=xq[:, ct * WA:(ct + 1) * WA],
                                              start=(ct == 0), stop=(ct == NT - 1))
                        m.then_inc(mmr, 1)
                    tensor.wait_ge(rv, c + 1)
                    for j in range(NT):
                        g = NT * c + j
                        if g >= 2:
                            tensor.wait_ge(fin, g - 1)
                        tensor.wait_ge(wolp[g % 2], 16 * (g // 2 + 1))
                        for dt in range(NT):
                            m = tensor.matmul(out=ps_o[j % 2][:, 0:WA],
                                              lhsT=wost[:, ((g % 2) * NT + dt) * 128:((g % 2) * NT + dt + 1) * 128],
                                              rhs=rwb[:, dt * WA:(dt + 1) * WA],
                                              start=(dt == 0), stop=(dt == NT - 1))
                        m.then_inc(mmo, 1)

            @blk.gpsimd
            def _(gpsimd):
                pass



FH = 4096
NFH = FH // 128

def p_rwkv_ffn(nc, semf, srcy, srcz, dst, prm, wvf_scr, tag):
    es = ExitStack()
    sem = lambda n: semf(n)
    with es:
        sb = lambda n, s, d: es.enter_context(nc.sbuf_tensor(tag + n, s, d))
        zt = sb("zt", [128, 2 * NT * WR], F32)
        z1t = sb("z1t", [128, NT * WR], F32)
        xb = sb("xb", [128, 2 * NT * WR], BF16)
        xq = sb("xq", [128, 2 * NT * WR], BF16)
        wmx = sb("wmx", [128, NT * (WR + 1)], BF16)
        kfb = sb("kfb", [128, NFH * WR], BF16)
        rlb = sb("rlb", [128, 2 * WR], BF16)
        sr2 = sb("sr2", [128, NT * WR], BF16)
        mu = sb("mu", [128, WR], F32)
        t1 = sb("t1", [128, WR], F32)
        t2 = sb("t2", [128, WR], F32)
        t3 = sb("t3", [128, WR], F32)
        ss = sb("ss", [128, WR], F32)
        epsv = sb("epsv", [128, 1], F32)
        eps2v = sb("eps2v", [128, 1], F32)
        ones = sb("ones", [128, 128], BF16)
        prmt = sb("prmt", [128, 4 * NT], F32)
        omk = sb("omk", [128, 2 * NT], F32)
        wstg = sb("wstg", [128, 2 * FH], F32)
        wvq = sb("wvq", [128, 2 * C], BF16)
        wkf_sb = sb("wkf_sb", [128, NT * FH], BF16)
        wrf_sb = sb("wrf_sb", [128, NT * C], BF16)
        wvstr = sb("wvstr", [128, 2 * NFH * 128], BF16)
        ps_sum = es.enter_context(nc.psum_tensor(tag + "pss", [128, WR], F32))
        ps_sq = es.enter_context(nc.psum_tensor(tag + "psq", [128, WR], F32))
        ps_p0 = es.enter_context(nc.psum_tensor(tag + "pp0", [128, WR], F32))
        ps_p1 = es.enter_context(nc.psum_tensor(tag + "pp1", [128, WR], F32))
        ps_o0 = es.enter_context(nc.psum_tensor(tag + "po0", [128, WR], F32))
        ps_o1 = es.enter_context(nc.psum_tensor(tag + "po1", [128, WR], F32))
        ps_p = (ps_p0, ps_p1)
        ps_o = (ps_o0, ps_o1)

        ldp = (sem("ld0"), sem("ld1")); ldz = sem("ldz")
        prs = sem("prs"); wcv = sem("wcv")
        wldp = (sem("wld0"), sem("wld1"))
        wvsp = (sem("wvs0"), sem("wvs1"))
        wvlp = (sem("wvl0"), sem("wvl1"))
        vprep = sem("vprep")
        cast = sem("cast"); sq = sem("sq"); mmst = sem("mmst")
        sA = sem("sA"); vA = sem("vA"); sB = sem("sB"); vB = sem("vB"); sC = sem("sC")
        xkr = sem("xkr"); xrr = sem("xrr")
        mmk2 = sem("mmk2"); mmr2 = sem("mmr2"); mmkv = sem("mmkv")
        rh = sem("rh"); rk = sem("rk"); kfv = sem("kfv")
        fin = sem("fin"); st = sem("st")

        def BLKF(c, i):
            off = (c % 2) * NT * WR
            return slice(off + i * WR, off + (i + 1) * WR)

        with nc.Block(tag) as blk:
            @blk.sync
            def _(sync):
                for j, p in enumerate([prm["ln2g"], prm["ln2b"], prm["fmixk"], prm["fmixr"]]):
                    sync.dma_start(out=prmt[:, j * NT:(j + 1) * NT], in_=p).then_inc(prs, 16)
                for kk in range(8):
                    if kk >= 2:
                        sync.wait_ge(wcv, kk - 1)
                    sync.dma_start(out=wstg[:, (kk % 2) * FH:(kk % 2 + 1) * FH],
                                   in_=prm["wkfT"][kk * 128:(kk + 1) * 128, :]).then_inc(wldp[kk % 2], 16)
                for j in range(NFH):
                    kk = 8 + j
                    sync.wait_ge(wcv, kk - 1)
                    sync.dma_start(out=wstg[:, (kk % 2) * FH:(kk % 2) * FH + C],
                                   in_=prm["wvfT"][j * 128:(j + 1) * 128, :]).then_inc(wldp[kk % 2], 16)
                    sync.wait_ge(wcv, kk + 1)
                    sync.dma_start(out=wvf_scr[j * 128:(j + 1) * 128, :],
                                   in_=wvq[:, (j % 2) * C:(j % 2 + 1) * C]).then_inc(wvsp[j % 2], 16)
                for j in range(NT):
                    kk = 40 + j
                    sync.wait_ge(wcv, kk - 1)
                    sync.dma_start(out=wstg[:, (kk % 2) * FH:(kk % 2) * FH + C],
                                   in_=prm["wrfT"][j * 128:(j + 1) * 128, :]).then_inc(wldp[kk % 2], 16)
                srcy3 = srcy.rearrange("(i p) t -> p i t", p=128)
                srcz3 = srcz.rearrange("(i p) t -> p i t", p=128)
                dst3 = dst.rearrange("(i p) t -> p i t", p=128)
                z1t3 = z1t[:].rearrange("p (i t) -> p i t", i=NT)
                wvf3 = wvf_scr.rearrange("(kh p) c -> p kh c", p=128)
                def emit_load(cc):
                    if cc >= 2:
                        sync.wait_ge(st, 16 * (cc - 1))
                    sync.dma_start(out=zt[:, (cc % 2) * NT * WR:(cc % 2 + 1) * NT * WR].rearrange(
                                       "p (i t) -> p i t", i=NT),
                                   in_=srcy3[:, :, cc * WR:(cc + 1) * WR]).then_inc(ldp[cc % 2], 16)

                emit_load(0)
                if NCHR > 1:
                    emit_load(1)
                for c in range(NCHR):
                    if c > 0:
                        sync.wait_ge(st, 16 * c)
                    sync.dma_start(out=z1t3, in_=srcz3[:, :, c * WR:(c + 1) * WR]).then_inc(ldz, 16)
                    for ct in range(NT):
                        g = c * NT + ct
                        if g >= 2:
                            sync.wait_ge(mmkv, g - 1)
                        if g == 0:
                            sync.wait_ge(wvsp[0], 16 * (NFH // 2))
                            sync.wait_ge(wvsp[1], 16 * (NFH // 2))
                        sync.dma_start(
                            out=wvstr[:, (g % 2) * NFH * 128:((g % 2) + 1) * NFH * 128].rearrange(
                                "p (kh c) -> p kh c", kh=NFH),
                            in_=wvf3[:, :, ct * 128:(ct + 1) * 128]).then_inc(wvlp[g % 2], 16)
                    sync.wait_ge(fin, NT * (c + 1))
                    sync.dma_start(out=dst3[:, :, c * WR:(c + 1) * WR], in_=z1t3).then_inc(st, 16)
                    if c + 2 < NCHR:
                        emit_load(c + 2)
                sync.wait_ge(st, 16 * NCHR)

            @blk.scalar
            def _(scalar):
                for kk in range(8):
                    scalar.wait_ge(wldp[kk % 2], 16 * (kk // 2 + 1))
                    scalar.copy(out=wkf_sb[:, kk * FH:(kk + 1) * FH],
                                in_=wstg[:, (kk % 2) * FH:(kk % 2 + 1) * FH]).then_inc(wcv, 1)
                for j in range(NFH):
                    kk = 8 + j
                    scalar.wait_ge(wldp[kk % 2], 16 * (kk // 2 + 1))
                    if j >= 2:
                        scalar.wait_ge(wvsp[j % 2], 16 * (j // 2))
                    scalar.copy(out=wvq[:, (j % 2) * C:(j % 2 + 1) * C],
                                in_=wstg[:, (kk % 2) * FH:(kk % 2) * FH + C]).then_inc(wcv, 1)
                for j in range(NT):
                    kk = 40 + j
                    scalar.wait_ge(wldp[kk % 2], 16 * (kk // 2 + 1))
                    scalar.copy(out=wrf_sb[:, j * C:(j + 1) * C],
                                in_=wstg[:, (kk % 2) * FH:(kk % 2) * FH + C]).then_inc(wcv, 1)
                for c in range(NCHR):
                    scalar.wait_ge(ldp[c % 2], 16 * (c // 2 + 1))
                    if c >= 2:
                        scalar.wait_ge(mmst, c - 1)
                    for i in range(NT):
                        ins = scalar.copy(out=xb[:, BLKF(c, i)], in_=zt[:, BLKF(c, i)])
                    ins.then_inc(cast, 1)
                    if c >= 2:
                        scalar.wait_ge(mmr2, NT * (c - 1))
                    for i in range(NT):
                        ins = scalar.activation(out=xq[:, BLKF(c, i)],
                                                in_=xb[:, BLKF(c, i)], func=AF.Square)
                    ins.then_inc(sq, 1)
                    scalar.wait_ge(mmst, c + 1)
                    scalar.activation(out=mu[:], in_=ps_sum[:, 0:WR], func=AF.Copy, scale=1.0 / C)
                    scalar.activation(out=t1[:], in_=ps_sq[:, 0:WR], func=AF.Copy,
                                      scale=1.0 / C).then_inc(sA, 1)
                    scalar.wait_ge(vA, c + 1)
                    scalar.activation(out=t2[:], in_=t1[:], func=AF.Ln, bias=eps2v[:, 0:1],
                                      scale=float(1.0 + EPS))
                    scalar.activation(out=ss[:], in_=t2[:], func=AF.Exp, scale=-0.5).then_inc(sC, 1)
                    if c > 0:
                        scalar.wait_ge(mmkv, NT * c)
                    for h in range(NFH):
                        scalar.wait_ge(mmk2, NFH * c + h + 1)
                        scalar.activation(out=rlb[:, (h % 2) * WR:(h % 2 + 1) * WR],
                                          in_=ps_p[h % 2][:, 0:WR], func=AF.Relu).then_inc(rh, 1)
                        scalar.activation(out=kfb[:, h * WR:(h + 1) * WR],
                                          in_=rlb[:, (h % 2) * WR:(h % 2 + 1) * WR],
                                          func=AF.Square).then_inc(kfv, 1)
                    for i in range(NT):
                        scalar.wait_ge(mmr2, NT * c + i + 1)
                        scalar.activation(out=sr2[:, i * WR:(i + 1) * WR],
                                          in_=ps_p[i % 2][:, 0:WR], func=AF.Sigmoid).then_inc(rk, 1)

            @blk.vector
            def _(vector):
                mixbuf = xq
                vector.memset(ones[:], 1.0)
                vector.memset(epsv[:], float(EPS))
                vector.memset(eps2v[:], float(EPS * EPS))
                for i in range(NT):
                    vector.memset(wmx[:, i * (WR + 1):i * (WR + 1) + 1], 0.0)
                vector.wait_ge(prs, 64)
                for j in range(2):
                    vector.tensor_scalar(out=omk[:, j * NT:(j + 1) * NT],
                                         in0=prmt[:, (2 + j) * NT:(3 + j) * NT],
                                         scalar1=-1.0, scalar2=1.0, op0=ALU.mult,
                                         op1=ALU.add).then_inc(vprep, 1)
                gcol = prmt[:, 0:NT]
                bcol = prmt[:, NT:2 * NT]
                def emit_prologue(cc):
                    vector.wait_ge(sA, cc + 1)
                    vector.tensor_tensor(out=t2[:], in0=mu[:], in1=mu[:], op=ALU.mult)
                    vector.tensor_tensor(out=t1[:], in0=t1[:], in1=t2[:],
                                         op=ALU.subtract).then_inc(vA, 1)
                    vector.wait_ge(sC, cc + 1)
                    vector.wait_ge(mmst, cc + 1)
                    for i in range(NT):
                        vector.tensor_tensor(out=t2[:], in0=zt[:, BLKF(cc, i)], in1=mu[:],
                                             op=ALU.subtract)
                        vector.tensor_tensor(out=t2[:], in0=t2[:], in1=ss[:], op=ALU.mult)
                        vector.tensor_scalar(out=wmx[:, i * (WR + 1) + 1:(i + 1) * (WR + 1)],
                                             in0=t2[:], scalar1=gcol[:, i:i + 1],
                                             scalar2=bcol[:, i:i + 1], op0=ALU.mult, op1=ALU.add)
                    for i in range(NT):
                        wcur = wmx[:, i * (WR + 1) + 1:(i + 1) * (WR + 1)]
                        wsh = wmx[:, i * (WR + 1):(i + 1) * (WR + 1) - 1]
                        vector.tensor_scalar(out=mixbuf[:, BLKF(cc, i)], in0=wsh,
                                             scalar1=omk[:, i:i + 1], scalar2=None, op0=ALU.mult)
                        ins = vector.scalar_tensor_tensor(out=mixbuf[:, BLKF(cc, i)],
                                                          in0=wcur, scalar=prmt[:, 2 * NT + i:2 * NT + i + 1],
                                                          in1=mixbuf[:, BLKF(cc, i)],
                                                          op0=ALU.mult, op1=ALU.add)
                    ins.then_inc(xkr, 1)

                emit_prologue(0)
                for c in range(NCHR):
                    vector.wait_ge(mmk2, NFH * (c + 1))
                    for i in range(NT):
                        wcur = wmx[:, i * (WR + 1) + 1:(i + 1) * (WR + 1)]
                        wsh = wmx[:, i * (WR + 1):(i + 1) * (WR + 1) - 1]
                        vector.tensor_scalar(out=mixbuf[:, BLKF(c, i)], in0=wsh,
                                             scalar1=omk[:, NT + i:NT + i + 1], scalar2=None, op0=ALU.mult)
                        ins = vector.scalar_tensor_tensor(out=mixbuf[:, BLKF(c, i)],
                                                          in0=wcur, scalar=prmt[:, 3 * NT + i:3 * NT + i + 1],
                                                          in1=mixbuf[:, BLKF(c, i)],
                                                          op0=ALU.mult, op1=ALU.add)
                    ins.then_inc(xrr, 1)
                    for i in range(NT):
                        vector.tensor_copy(out=wmx[:, i * (WR + 1):i * (WR + 1) + 1],
                                           in_=wmx[:, (i + 1) * (WR + 1) - 1:(i + 1) * (WR + 1)])
                    if c + 1 < NCHR:
                        emit_prologue(c + 1)
                    vector.wait_ge(ldz, 16 * (c + 1))
                    for ct in range(NT):
                        vector.wait_ge(mmkv, NT * c + ct + 1)
                        vector.wait_ge(rk, NT * c + ct + 1)
                        sl = slice(ct * WR, (ct + 1) * WR)
                        vector.tensor_tensor(out=t2[:], in0=ps_o[ct % 2][:, 0:WR], in1=sr2[:, sl],
                                             op=ALU.mult)
                        vector.tensor_tensor(out=z1t[:, sl], in0=z1t[:, sl], in1=zt[:, BLKF(c, ct)],
                                             op=ALU.add)
                        vector.tensor_tensor(out=z1t[:, sl], in0=z1t[:, sl], in1=t2[:],
                                             op=ALU.add).then_inc(fin, 1)

            @blk.tensor
            def _(tensor):
                tensor.wait_ge(wcv, 48)
                tensor.wait_ge(vprep, 2)

                def emit_stats(cc):
                    tensor.wait_ge(cast, cc + 1)
                    tensor.wait_ge(sq, cc + 1)
                    if cc > 0:
                        tensor.wait_ge(sA, cc)
                    for i in range(NT):
                        tensor.matmul(out=ps_sum[:, 0:WR], lhsT=ones[:, 0:128], rhs=xb[:, BLKF(cc, i)],
                                      start=(i == 0), stop=(i == NT - 1))
                    for i in range(NT):
                        m = tensor.matmul(out=ps_sq[:, 0:WR], lhsT=ones[:, 0:128], rhs=xq[:, BLKF(cc, i)],
                                          start=(i == 0), stop=(i == NT - 1))
                    m.then_inc(mmst, 1)

                emit_stats(0)
                for c in range(NCHR):
                    tensor.wait_ge(xkr, c + 1)
                    for h in range(NFH):
                        if h >= 2:
                            tensor.wait_ge(rh, NFH * c + h - 1)
                        elif c > 0:
                            tensor.wait_ge(rk, NT * c + h - 1)
                        for ct in range(NT):
                            m = tensor.matmul(out=ps_p[h % 2][:, 0:WR],
                                              lhsT=wkf_sb[:, ct * FH + h * 128:ct * FH + (h + 1) * 128],
                                              rhs=xq[:, BLKF(c, ct)],
                                              start=(ct == 0), stop=(ct == NT - 1))
                        m.then_inc(mmk2, 1)
                    tensor.wait_ge(xrr, c + 1)
                    for i in range(NT):
                        if i < 2:
                            tensor.wait_ge(rh, NFH * c + 31 + i)
                        else:
                            tensor.wait_ge(rk, NT * c + i - 1)
                        for ct in range(NT):
                            m = tensor.matmul(out=ps_p[i % 2][:, 0:WR],
                                              lhsT=wrf_sb[:, ct * C + i * 128:ct * C + (i + 1) * 128],
                                              rhs=xq[:, BLKF(c, ct)],
                                              start=(ct == 0), stop=(ct == NT - 1))
                        m.then_inc(mmr2, 1)
                    if c + 1 < NCHR:
                        emit_stats(c + 1)
                    tensor.wait_ge(kfv, NFH * (c + 1))
                    for ct in range(NT):
                        g = c * NT + ct
                        if g >= 2:
                            tensor.wait_ge(fin, g - 1)
                        tensor.wait_ge(wvlp[g % 2], 16 * (g // 2 + 1))
                        for kh in range(NFH):
                            m = tensor.matmul(out=ps_o[ct % 2][:, 0:WR],
                                              lhsT=wvstr[:, ((g % 2) * NFH + kh) * 128:((g % 2) * NFH + kh + 1) * 128],
                                              rhs=kfb[:, kh * WR:(kh + 1) * WR],
                                              start=(kh == 0), stop=(kh == NFH - 1))
                        m.then_inc(mmkv, 1)

            @blk.gpsimd
            def _(gpsimd):
                pass



def p_l1attn(nc, semf, src, dst, prm, kscr, vscr, tag):
    NP = 8

    def G(c):
        return 4 * c * (c + 1)

    es = ExitStack()
    sem = lambda n: semf(n)
    with es:
        sb = lambda n, s, d: es.enter_context(nc.sbuf_tensor(tag + n, s, d))
        zt = sb("zt", [128, NT * W], F32)
        xb = sb("xb", [128, NT * W], BF16)
        xq = sb("xq", [128, NT * W], BF16)
        xnb = xq
        mu = sb("mu", [128, W], F32)
        t1 = sb("t1", [128, W], F32)
        t2 = sb("t2", [128, W], F32)
        t3 = sb("t3", [128, W], F32)
        epsv = sb("epsv", [128, 1], F32)
        ones = sb("ones", [128, 128], BF16)
        onesf = sb("onesf", [128, 128], F32)
        qstg = sb("qstg", [128, 2 * W], BF16)
        kstg = sb("kstg", [128, 2 * W], BF16)
        rtp = sb("rtp", [128, W], F32)
        qrope = sb("qrope", [128, NT * W], BF16)
        krope = sb("krope", [128, NT * W], BF16)
        rcosb = sb("rcosb", [128, W], F32)
        rsinb = sb("rsinb", [128, W], F32)
        bvrow = sb("bvrow", [1, C], F32)
        bvbc = sb("bvbc", [128, C], F32)
        vtmp = sb("vtmp", [128, 2 * W], BF16)
        vchunk = sb("vchunk", [128, 4 * 1040], BF16)
        kstr = sb("kstr", [128, 2 * W], BF16)
        vstr = sb("vstr", [128, 2 * 4 * 130], BF16)
        ebuf = sb("ebuf", [128, 16 * W], BF16)
        dbuf = sb("dbuf", [128, 2 * W], F32)
        btmp = sb("btmp", [128, 2 * W], BF16)
        ctxb = sb("ctxb", [128, NT * W], BF16)
        wtmp = sb("wtmp", [128, 2 * W], F32)
        prmt = sb("prmt", [128, 3 * NT], F32)
        cmk = sb("cmk", [128, 4 * W], BF16)
        sS = sb("sS", [128, 128], BF16)
        wstg = sb("wstg", [128, 2 * C], F32)
        wq_sb = sb("wq_sb", [128, NT * C], BF16)
        wk_sb = sb("wk_sb", [128, NT * C], BF16)
        wv_sb = sb("wv_sb", [128, NT * C], BF16)
        wo_sb = sb("wo_sb", [128, NT * C], BF16)
        ps_sum = es.enter_context(nc.psum_tensor(tag + "pss", [128, W], F32))
        ps_sq = es.enter_context(nc.psum_tensor(tag + "psq", [128, W], F32))
        ps_a0 = es.enter_context(nc.psum_tensor(tag + "pa0", [128, W], F32))
        ps_a1 = es.enter_context(nc.psum_tensor(tag + "pa1", [128, W], F32))
        ps_b0 = es.enter_context(nc.psum_tensor(tag + "pb0", [128, W], F32))
        ps_b1 = es.enter_context(nc.psum_tensor(tag + "pb1", [128, W], F32))
        ps_ca = es.enter_context(nc.psum_tensor(tag + "pca", [128, W], F32))
        ps_cb = es.enter_context(nc.psum_tensor(tag + "pcb", [128, W], F32))
        ps_a = (ps_a0, ps_a1)
        ps_b = (ps_b0, ps_b1)
        ps_v = (ps_ca, ps_cb)

        ld = sem("ld"); prs = sem("prs"); wcv = sem("wcv")
        wldp = (sem("wld0"), sem("wld1"))
        cast = sem("cast"); sq = sem("sq"); mmst = sem("mmst")
        sA = sem("sA"); vA = sem("vA"); sB = sem("sB")
        xn_s = sem("xn")
        mmq = sem("mmq"); mmk = sem("mmk"); mmv = sem("mmv")
        msw = sem("msw")
        qev = sem("qev"); kev = sem("kev"); vev = sem("vev")
        qrp = sem("qrp"); krp = sem("krp"); vcp = sem("vcp")
        kst = sem("kst"); vst = sem("vst"); rtl = sem("rtl")
        kslp = (sem("ksl0"), sem("ksl1"))
        vslp = (sem("vsl0"), sem("vsl1"))
        exv = sem("exv"); dmk = sem("dmk"); pvc = sem("pvc"); scg = sem("scg")
        cte = sem("cte"); rcv = sem("rcv"); bcv = sem("bcv"); ctn = sem("ctn")
        cbdp = (sem("cbd0"), sem("cbd1"))
        mwo = sem("mwo"); woev = sem("woev"); bvv = sem("bvv"); bve = sem("bve")
        fin = sem("fin"); st = sem("st")

        w_aps = [prm["qwT"], prm["kwT"], prm["vwT"], prm["owT"]]
        w_sbs = [wq_sb, wk_sb, wv_sb, wo_sb]

        with nc.Block(tag) as blk:
            @blk.sync
            def _(sync):
                for j, p in enumerate([prm["bq"], prm["bk"], prm["bo"]]):
                    sync.dma_start(out=prmt[:, j * NT:(j + 1) * NT], in_=p).then_inc(prs, 16)
                sync.dma_start(out=bvrow[:], in_=prm["bv_row"]).then_inc(prs, 16)
                sync.dma_start(out=cmk[:], in_=prm["cmask"]).then_inc(prs, 16)
                for kk in range(33):
                    if kk >= 2:
                        sync.wait_ge(wcv, kk - 1)
                    src_ap = (w_aps[kk // 8][(kk % 8) * 128:(kk % 8 + 1) * 128, :]
                              if kk < 32 else prm["ropeS"])
                    sync.dma_start(out=wstg[:, (kk % 2) * C:(kk % 2) * C + src_ap.shape[1]],
                                   in_=src_ap).then_inc(wldp[kk % 2], 16)
                src3 = src.rearrange("(i p) t -> p i t", p=128)
                dst3 = dst.rearrange("(i p) t -> p i t", p=128)
                zt3 = zt[:].rearrange("p (i t) -> p i t", i=NT)
                kscr3 = kscr.rearrange("(i p) t -> p i t", p=128)
                krope3 = krope[:].rearrange("p (i t) -> p i t", i=NT)
                vchunk3 = vchunk[:].rearrange("p (tt c) -> p tt c", tt=4)
                for c in range(NCH):
                    if c > 0:
                        sync.wait_ge(st, 16 * c)
                    sync.dma_start(out=zt3, in_=src3[:, :, c * W:(c + 1) * W]).then_inc(ld, 16)
                    if c > 0:
                        sync.wait_ge(krp, NT * c)
                    sync.dma_start(out=rcosb[:], in_=prm["rcos"][:, c * W:(c + 1) * W]).then_inc(rtl, 16)
                    sync.dma_start(out=rsinb[:], in_=prm["rsin"][:, c * W:(c + 1) * W]).then_inc(rtl, 16)
                    sync.wait_ge(krp, NT * (c + 1))
                    sync.dma_start(out=kscr3[:, :, c * W:(c + 1) * W], in_=krope3).then_inc(kst, 16)
                    sync.wait_ge(vcp, c + 1)
                    sync.dma_start(out=vscr[c * W:(c + 1) * W, :].rearrange("(tt p) c -> p tt c", p=128),
                                   in_=vchunk3).then_inc(vst, 16)
                    for j in range(NP):
                        for kc in range(c + 1):
                            g = G(c) + j * (c + 1) + kc
                            sync.wait_ge(kst, 16 * (kc + 1))
                            sync.wait_ge(vst, 16 * (kc + 1))
                            if g >= 2:
                                sync.wait_ge(scg, 8 * (g - 1))
                            sync.dma_start(out=kstr[:, (g % 2) * W:(g % 2 + 1) * W],
                                           in_=kscr[j * 128:(j + 1) * 128, kc * W:(kc + 1) * W]
                                           ).then_inc(kslp[g % 2], 16)
                            if g >= 2:
                                sync.wait_ge(pvc, g - 1)
                            sync.dma_start(
                                out=vstr[:, (g % 2) * 4 * 130:((g % 2) + 1) * 4 * 130].rearrange(
                                    "p (tt c) -> p tt c", tt=4),
                                in_=vscr[kc * W:(kc + 1) * W, j * 130:(j + 1) * 130].rearrange(
                                    "(tt p) c -> p tt c", p=128)).then_inc(vslp[g % 2], 16)
                        sync.wait_ge(cte, 4 * (NP * c + j) + 4)
                        sync.dma_start(out=ctxb[64:128, j * W:(j + 1) * W],
                                       in_=btmp[0:64, (j % 2) * W:(j % 2 + 1) * W]
                                       ).then_inc(cbdp[j % 2], 16)
                    sync.wait_ge(fin, NT * (c + 1))
                    sync.dma_start(out=dst3[:, :, c * W:(c + 1) * W], in_=zt3).then_inc(st, 16)
                sync.wait_ge(st, 16 * NCH)
                sync.wait_ge(kst, 16 * NCH)
                sync.wait_ge(vst, 16 * NCH)

            @blk.scalar
            def _(scalar):
                for kk in range(33):
                    scalar.wait_ge(wldp[kk % 2], 16 * (kk // 2 + 1))
                    out_sb = (w_sbs[kk // 8][:, (kk % 8) * C:(kk % 8 + 1) * C] if kk < 32
                              else sS[:])
                    ncol = C if kk < 32 else 128
                    scalar.copy(out=out_sb,
                                in_=wstg[:, (kk % 2) * C:(kk % 2) * C + ncol]).then_inc(wcv, 1)
                for c in range(NCH):
                    scalar.wait_ge(ld, 16 * (c + 1))
                    for i in range(NT):
                        ins = scalar.copy(out=xb[:, i * W:(i + 1) * W], in_=zt[:, i * W:(i + 1) * W])
                    ins.then_inc(cast, 1)
                    scalar.wait_ge(mmst, c + 1)
                    scalar.activation(out=mu[:], in_=ps_sum[:, 0:W], func=AF.Copy, scale=1.0 / C)
                    scalar.activation(out=t1[:], in_=ps_sq[:, 0:W], func=AF.Copy,
                                      scale=1.0 / C).then_inc(sA, 1)
                    scalar.wait_ge(vA, c + 1)
                    scalar.activation(out=t2[:], in_=t1[:], func=AF.Ln, bias=epsv[:, 0:1])
                    scalar.activation(out=t3[:], in_=t2[:], func=AF.Exp, scale=-0.5).then_inc(sB, 1)
                    for arr, mm_s, ev_s, rp_s, stg, pb in ((0, mmq, qev, qrp, qstg, 0),
                                                           (1, mmk, kev, krp, kstg, NT)):
                        for dt in range(NT):
                            scalar.wait_ge(mm_s, NT * c + dt + 1)
                            if dt >= 2:
                                scalar.wait_ge(rp_s, NT * c + dt - 1)
                            scalar.activation(out=stg[:, (dt % 2) * W:(dt % 2 + 1) * W],
                                              in_=(ps_a if arr == 0 else ps_b)[dt % 2][:, 0:W],
                                              func=AF.Identity,
                                              bias=prmt[:, pb + dt:pb + dt + 1]).then_inc(ev_s, 1)
                    for j in range(NP):
                        for kc in range(c + 1):
                            g = G(c) + j * (c + 1) + kc
                            if g >= 2:
                                scalar.wait_ge(pvc, g - 1)
                            for sub in range(4):
                                for hh, psx in ((0, ps_a), (1, ps_b)):
                                    slot = (g % 2) * 8 + 2 * sub + hh
                                    scalar.wait_ge(scg, 8 * g + 2 * sub + hh + 1)
                                    scalar.activation(out=ebuf[:, slot * W:(slot + 1) * W],
                                                      in_=psx[sub % 2][:, 0:W], func=AF.Exp,
                                                      scale=0.125).then_inc(exv, 1)
                        P4 = 4 * (NP * c + j)
                        gl = G(c) + j * (c + 1) + c
                        scalar.wait_ge(pvc, gl + 1)
                        if NP * c + j >= 2:
                            scalar.wait_ge(cbdp[j % 2], 16 * ((NP * c + j) // 2))
                        scalar.copy(out=dbuf[64:65, 0:W], in_=ps_ca[64:65, 0:W]).then_inc(cte, 1)
                        scalar.copy(out=dbuf[64:65, W:2 * W], in_=ps_cb[64:65, 0:W]).then_inc(cte, 1)
                        scalar.copy(out=ctxb[0:64, j * W:(j + 1) * W],
                                    in_=ps_ca[0:64, 0:W]).then_inc(cte, 1)
                        scalar.copy(out=btmp[0:64, (j % 2) * W:(j % 2 + 1) * W],
                                    in_=ps_cb[0:64, 0:W]).then_inc(cte, 1)
                    for jo in range(NT):
                        scalar.wait_ge(mwo, NT * c + jo + 1)
                        if NT * c + jo >= 2:
                            scalar.wait_ge(fin, NT * c + jo - 1)
                        scalar.activation(out=wtmp[:, (jo % 2) * W:(jo % 2 + 1) * W],
                                          in_=ps_a[jo % 2][:, 0:W], func=AF.Identity,
                                          bias=prmt[:, 2 * NT + jo:2 * NT + jo + 1]).then_inc(woev, 1)

            @blk.vector
            def _(vector):
                vector.memset(ones[:], 1.0)
                vector.memset(onesf[:], 1.0)
                vector.memset(epsv[:], float(EPS))
                vector.memset(vchunk[:], 1.0)
                for half in range(2):
                    vector.wait_ge(bvv, half + 1)
                    vector.tensor_copy(out=bvbc[:, half * W:(half + 1) * W],
                                       in_=ps_a[half][:, 0:W]).then_inc(bve, 1)
                for c in range(NCH):
                    vector.wait_ge(cast, c + 1)
                    for i in range(NT):
                        ins = vector.tensor_tensor(out=xq[:, i * W:(i + 1) * W],
                                                   in0=xb[:, i * W:(i + 1) * W],
                                                   in1=xb[:, i * W:(i + 1) * W], op=ALU.mult)
                    ins.then_inc(sq, 1)
                    vector.wait_ge(sA, c + 1)
                    vector.tensor_tensor(out=t2[:], in0=mu[:], in1=mu[:], op=ALU.mult)
                    vector.tensor_tensor(out=t1[:], in0=t1[:], in1=t2[:],
                                         op=ALU.subtract).then_inc(vA, 1)
                    vector.wait_ge(sB, c + 1)
                    vector.wait_ge(mmst, c + 1)
                    for i in range(NT):
                        sl = slice(i * W, (i + 1) * W)
                        vector.tensor_tensor(out=t2[:], in0=zt[:, sl], in1=mu[:], op=ALU.subtract)
                        ins = vector.tensor_tensor(out=xnb[:, sl], in0=t2[:], in1=t3[:], op=ALU.mult)
                    ins.then_inc(xn_s, 1)
                    vector.wait_ge(rtl, 32 * (c + 1))
                    if c > 0:
                        vector.wait_ge(kst, 16 * c)
                    for arr, stg, outb, rp_s in ((0, qstg, qrope, qrp), (1, kstg, krope, krp)):
                        for dt in range(NT):
                            vector.wait_ge(msw, 16 * c + arr * NT + dt + 1)
                            ssl = slice((dt % 2) * W, (dt % 2 + 1) * W)
                            bsl = slice(dt * W, (dt + 1) * W)
                            psx = (ps_a if arr == 0 else ps_b)[dt % 2]
                            vector.tensor_tensor(out=rtp[:], in0=stg[:, ssl], in1=rcosb[:], op=ALU.mult)
                            vector.tensor_tensor(out=t2[:], in0=psx[:, 0:W], in1=rsinb[:], op=ALU.mult)
                            vector.tensor_tensor(out=outb[:, bsl], in0=rtp[:], in1=t2[:],
                                                 op=ALU.add).then_inc(rp_s, 1)
                    if c > 0:
                        vector.wait_ge(vst, 16 * c)
                    for idx in range(NT):
                        tt, eh = idx // 2, idx % 2
                        vector.wait_ge(mmv, NT * c + idx + 1)
                        vector.tensor_tensor(out=vtmp[:, (idx % 2) * W:(idx % 2 + 1) * W],
                                             in0=ps_v[idx % 2][:, 0:W],
                                             in1=bvbc[:, eh * W:(eh + 1) * W], op=ALU.add).then_inc(vev, 1)
                        for h8 in range(8):
                            h = eh * 8 + h8
                            ins = vector.tensor_copy(
                                out=vchunk[:, tt * 1040 + h * 65:tt * 1040 + h * 65 + 64],
                                in_=vtmp[:, (idx % 2) * W + h8 * 64:(idx % 2) * W + (h8 + 1) * 64])
                    ins.then_inc(vcp, 1)
                    for j in range(NP):
                        gd = G(c) + j * (c + 1) + c
                        for sub in range(4):
                            for hh in range(2):
                                slot = (gd % 2) * 8 + 2 * sub + hh
                                vector.wait_ge(exv, 8 * gd + 2 * sub + hh + 1)
                                vector.tensor_tensor(out=ebuf[:, slot * W:(slot + 1) * W],
                                                     in0=ebuf[:, slot * W:(slot + 1) * W],
                                                     in1=cmk[:, sub * W:(sub + 1) * W],
                                                     op=ALU.mult).then_inc(dmk, 1)
                        P4 = 4 * (NP * c + j)
                        vector.wait_ge(cte, P4 + 1)
                        vector.reciprocal(out=dbuf[64:65, 0:W], in_=dbuf[64:65, 0:W]).then_inc(rcv, 1)
                        vector.wait_ge(cte, P4 + 2)
                        vector.reciprocal(out=dbuf[64:65, W:2 * W], in_=dbuf[64:65, W:2 * W]).then_inc(rcv, 1)
                        vector.wait_ge(bcv, 2 * (NP * c + j) + 1)
                        vector.wait_ge(cte, P4 + 3)
                        vector.tensor_tensor(out=ctxb[0:64, j * W:(j + 1) * W],
                                             in0=ctxb[0:64, j * W:(j + 1) * W],
                                             in1=ps_sum[0:64, 0:W], op=ALU.mult).then_inc(ctn, 1)
                        vector.wait_ge(bcv, 2 * (NP * c + j) + 2)
                        vector.wait_ge(cbdp[j % 2], 16 * ((NP * c + j) // 2 + 1))
                        vector.tensor_tensor(out=ctxb[64:128, j * W:(j + 1) * W],
                                             in0=ctxb[64:128, j * W:(j + 1) * W],
                                             in1=ps_sq[64:128, 0:W], op=ALU.mult).then_inc(ctn, 1)
                    for jo in range(NT):
                        vector.wait_ge(woev, NT * c + jo + 1)
                        vector.tensor_tensor(out=zt[:, jo * W:(jo + 1) * W],
                                             in0=zt[:, jo * W:(jo + 1) * W],
                                             in1=wtmp[:, (jo % 2) * W:(jo % 2 + 1) * W],
                                             op=ALU.add).then_inc(fin, 1)

            @blk.tensor
            def _(tensor):
                tensor.wait_ge(wcv, 33)
                tensor.wait_ge(prs, 80)
                for half in range(2):
                    tensor.matmul(out=ps_a[half][:, 0:W], lhsT=onesf[0:1, 0:128],
                                  rhs=bvrow[0:1, half * W:(half + 1) * W],
                                  start=True, stop=True).then_inc(bvv, 1)
                for c in range(NCH):
                    tensor.wait_ge(cast, c + 1)
                    tensor.wait_ge(sq, c + 1)
                    if c > 0:
                        tensor.wait_ge(sA, c)
                        tensor.wait_ge(ctn, 16 * c)
                    m = ln_stats_chunk_tensor(tensor, ones, xb, xq, ps_sum, ps_sq)
                    m.then_inc(mmst, 1)
                    tensor.wait_ge(xn_s, c + 1)
                    for arr, w_sb, mm_s, ev_s, rp_s, psp in ((0, wq_sb, mmq, qev, qrp, ps_a),
                                                             (1, wk_sb, mmk, kev, krp, ps_b)):
                        for dt in range(NT):
                            if dt >= 2:
                                tensor.wait_ge(rp_s, NT * c + dt - 1)
                            elif c > 0:
                                if arr == 0:
                                    tensor.wait_ge(woev, NT * c)
                                else:
                                    tensor.wait_ge(exv, 8 * G(c))
                            elif c == 0 and arr == 0:
                                tensor.wait_ge(bve, 2)
                            for ct in range(NT):
                                m = tensor.matmul(out=psp[dt % 2][:, 0:W],
                                                  lhsT=w_sb[:, ct * C + dt * 128:ct * C + (dt + 1) * 128],
                                                  rhs=xnb[:, ct * W:(ct + 1) * W],
                                                  start=(ct == 0), stop=(ct == NT - 1))
                            m.then_inc(mm_s, 1)
                            stg = qstg if arr == 0 else kstg
                            tensor.wait_ge(ev_s, NT * c + dt + 1)
                            tensor.matmul(out=psp[dt % 2][:, 0:W], lhsT=sS[:],
                                          rhs=stg[:, (dt % 2) * W:(dt % 2 + 1) * W],
                                          start=True, stop=True).then_inc(msw, 1)
                    for idx in range(NT):
                        tt, eh = idx // 2, idx % 2
                        if idx >= 2:
                            tensor.wait_ge(vev, NT * c + idx - 1)
                        elif c > 0:
                            tensor.wait_ge(cte, 32 * c)
                        psx = ps_v[idx % 2]
                        for ct in range(NT):
                            m = tensor.matmul(out=psx[:, 0:W],
                                              lhsT=xnb[:, ct * W + tt * 128:ct * W + (tt + 1) * 128],
                                              rhs=wv_sb[:, ct * C + eh * W:ct * C + (eh + 1) * W],
                                              start=(ct == 0), stop=(ct == NT - 1))
                        m.then_inc(mmv, 1)
                    for j in range(NP):
                        for kc in range(c + 1):
                            g = G(c) + j * (c + 1) + kc
                            tensor.wait_ge(kslp[g % 2], 16 * (g // 2 + 1))
                            tensor.wait_ge(vslp[g % 2], 16 * (g // 2 + 1))
                            for sub in range(4):
                                for hh, psx, tp in ((0, ps_a, 0), (1, ps_b, 64)):
                                    war = 8 * g + 2 * sub + hh - 3
                                    if kc == 0 and j == 0 and sub < 2:
                                        tensor.wait_ge(qrp, NT * (c + 1))
                                        tensor.wait_ge(krp, NT * (c + 1))
                                    elif war >= 1:
                                        tensor.wait_ge(exv, war)
                                    tensor.matmul(
                                        out=psx[sub % 2][:, 0:W],
                                        lhsT=kstr[tp:tp + 64, (g % 2) * W + sub * 128:(g % 2) * W + (sub + 1) * 128],
                                        rhs=qrope[tp:tp + 64, j * W:(j + 1) * W],
                                        start=True, stop=True,
                                        tile_position=(tp, 0)).then_inc(scg, 1)
                            for sub in range(4):
                                for hh, psc in ((0, ps_ca), (1, ps_cb)):
                                    slot = (g % 2) * 8 + 2 * sub + hh
                                    if kc == c:
                                        tensor.wait_ge(dmk, 8 * (NP * c + j) + 2 * sub + hh + 1)
                                    else:
                                        tensor.wait_ge(exv, 8 * g + 2 * sub + hh + 1)
                                    if kc == 0 and sub == 0 and hh == 0:
                                        if j > 0:
                                            tensor.wait_ge(cte, 4 * (NP * c + j))
                                        else:
                                            tensor.wait_ge(vev, NT * (c + 1))
                                    m = tensor.matmul(
                                        out=psc[0:65, 0:W],
                                        lhsT=vstr[:, ((g % 2) * 4 + sub) * 130 + hh * 65:
                                                  ((g % 2) * 4 + sub) * 130 + (hh + 1) * 65],
                                        rhs=ebuf[:, slot * W:(slot + 1) * W],
                                        start=(kc == 0 and sub == 0), stop=(kc == c and sub == 3))
                                    if sub == 3 and hh == 1:
                                        m.then_inc(pvc, 1)
                        for hh in range(2):
                            tensor.wait_ge(rcv, 2 * (NP * c + j) + hh + 1)
                            if j > 0:
                                tensor.wait_ge(ctn, 16 * c + 2 * j)
                            elif c > 0:
                                tensor.wait_ge(ctn, 16 * c)
                            if hh == 0:
                                tensor.matmul(out=ps_sum[0:64, 0:W], lhsT=onesf[64:65, 0:64],
                                              rhs=dbuf[64:65, 0:W], start=True, stop=True,
                                              tile_position=(64, 0)).then_inc(bcv, 1)
                            else:
                                tensor.matmul(out=ps_sq[64:128, 0:W], lhsT=onesf[64:65, 0:64],
                                              rhs=dbuf[64:65, W:2 * W], start=True, stop=True,
                                              tile_position=(64, 64)).then_inc(bcv, 1)
                    tensor.wait_ge(ctn, 16 * (c + 1))
                    tensor.wait_ge(exv, 8 * G(c + 1))
                    for jo in range(NT):
                        if NT * c + jo >= 2:
                            tensor.wait_ge(woev, NT * c + jo - 1)
                        for et in range(NT):
                            m = tensor.matmul(out=ps_a[jo % 2][:, 0:W],
                                              lhsT=wo_sb[:, et * C + jo * 128:et * C + (jo + 1) * 128],
                                              rhs=ctxb[:, et * W:(et + 1) * W],
                                              start=(et == 0), stop=(et == NT - 1))
                        m.then_inc(mwo, 1)

            @blk.gpsimd
            def _(gpsimd):
                pass



def _vec_tile(v):
    v = np.asarray(v).reshape(-1)
    return np.ascontiguousarray(v.reshape(-1, 128).T.astype(np.float32))


def _build_program(phases):
    nc = bass.Bass("TRN2", target_bir_lowering=False, detect_race_conditions=False)
    spec = {
        "x": ([C, T], F32, "ExternalInput"),
        "x2": ([C, T], F32, "ExternalInput"),
        "out": ([C, T], F32, "ExternalOutput"),
        "X": ([C, T], F32, "Internal"),
        "Y": ([C, T], F32, "Internal"),
        "l0f_g": ([128, NT], F32, "ExternalInput"),
        "l0f_b": ([128, NT], F32, "ExternalInput"),
        "wgT": ([C, HID], I32, "ExternalInput"),
        "wuT": ([C, HID], I32, "ExternalInput"),
        "wdT": ([HID, C], I32, "ExternalInput"),
        "WDscr": ([HID, C], BF16, "Internal"),
        "WUscr": ([C, HID], BF16, "Internal"),
        "WOscr": ([C, C], BF16, "Internal"),
        "wkT": ([C, C], F32, "ExternalInput"),
        "wvT": ([C, C], F32, "ExternalInput"),
        "wrT": ([C, C], F32, "ExternalInput"),
        "woT": ([C, C], F32, "ExternalInput"),
        "wkfT": ([C, FH], F32, "ExternalInput"),
        "wvfT": ([FH, C], F32, "ExternalInput"),
        "wrfT": ([C, C], F32, "ExternalInput"),
        "WVFscr": ([FH, C], BF16, "Internal"),
        "qwT": ([C, C], F32, "ExternalInput"),
        "kwT": ([C, C], F32, "ExternalInput"),
        "vwT": ([C, C], F32, "ExternalInput"),
        "owT": ([C, C], F32, "ExternalInput"),
        "rcos": ([128, T], F32, "ExternalInput"),
        "rsin": ([128, T], F32, "ExternalInput"),
        "ropeS": ([128, 128], F32, "ExternalInput"),
        "bv_row": ([1, C], F32, "ExternalInput"),
        "cmask": ([128, 4 * W], BF16, "ExternalInput"),
        "Kscr": ([C, T], BF16, "Internal"),
        "Vscr": ([T, 1040], BF16, "Internal"),
    }
    for n in ["ln1g", "ln1b", "mixk", "mixv", "mixr", "td", "tf",
              "ln2g", "ln2b", "fmixk", "fmixr", "bq", "bk", "bv", "bo"]:
        spec[n] = ([128, NT], F32, "ExternalInput")
    ap = {}

    def get(k):
        if k not in ap:
            sh, dt_, kind = spec[k]
            ap[k] = nc.dram_tensor(k, sh, dt_, kind=kind).ap()
        return ap[k]

    es_prog = ExitStack()
    with es_prog:
        NPOOL = 48
        pool = [es_prog.enter_context(nc.semaphore(f"pool{i}")) for i in range(NPOOL)]

        def make_semf():
            idx = [0]

            def semf(name):
                h = pool[idx[0]]
                idx[0] += 1
                assert idx[0] <= NPOOL, "semaphore pool exhausted"
                return h
            return semf

        def fence(k):
            with nc.Block(f"fence{k}") as fb:
                @fb.sync
                def _(sync):
                    for sh in pool:
                        sync.sem_clear(sh)

                @fb.scalar
                def _(scalar):
                    pass

                @fb.vector
                def _(vector):
                    pass

                @fb.tensor
                def _(tensor):
                    pass

                @fb.gpsimd
                def _(gpsimd):
                    pass

        for pi, ph in enumerate(phases):
            name = ph[0]
            tagp = f"ph{pi}_"
            fence(pi)
            semf = make_semf()
            if name == "l0ffn":
                p_l0ffn(nc, semf, get(ph[1]), get(ph[2]), get("l0f_g"), get("l0f_b"), tag=tagp)
            elif name == "l1ffn":
                p_l1ffn(nc, semf, get(ph[1]), get(ph[2]), get("wgT"), get("wuT"), get("wdT"),
                        get("WDscr"), get("WUscr"), tag=tagp)
            elif name == "rwkv_att":
                prm = {n: get(n) for n in ["wkT", "wvT", "wrT", "woT", "ln1g", "ln1b",
                                           "mixk", "mixv", "mixr", "td", "tf"]}
                p_rwkv_att(nc, semf, get(ph[1]), get(ph[2]), prm, get("WOscr"), tag=tagp)
            elif name == "l1attn":
                prm = {n: get(n) for n in ["qwT", "kwT", "vwT", "owT", "bq", "bk", "bo",
                                           "bv_row", "rcos", "rsin", "ropeS", "cmask"]}
                p_l1attn(nc, semf, get(ph[1]), get(ph[2]), prm, get("Kscr"), get("Vscr"),
                         tag=tagp)
            elif name == "rwkv_ffn":
                prm = {n: get(n) for n in ["wkfT", "wvfT", "wrfT", "ln2g", "ln2b",
                                           "fmixk", "fmixr"]}
                p_rwkv_ffn(nc, semf, get(ph[1]), get(ph[2]), get(ph[3]), prm,
                           get("WVFscr"), tag=tagp)
    return nc, set(ap.keys())


def _host_inputs(inputs):
    assert inputs["L0_ffn_attn_mask"].shape == (T, T)
    assert inputs["L0_ffn_Wg"].shape == (C, 16)
    assert inputs["L0_ffn_Wq"].shape[0] * inputs["L0_ffn_Wq"].shape[2] == 16 * 64
    assert inputs["L0_ffn_bq"].shape == (16, 64)
    assert inputs["L0_ffn_Wkv"].shape == (16, C, 128)
    assert inputs["L0_ffn_bkv"].shape == (16, 128)
    assert inputs["L0_ffn_Wo"].shape == (16, 64, C)
    d = {}
    d["l0f_g"] = _vec_tile(inputs["L0_ffn_ln_g"])
    d["l0f_b"] = _vec_tile(inputs["L0_ffn_ln_b"])
    d["wkT"] = np.ascontiguousarray(np.asarray(inputs["L0_attn_Wk"], dtype=np.float32).T)
    d["wvT"] = np.ascontiguousarray(np.asarray(inputs["L0_attn_Wv"], dtype=np.float32).T)
    d["wrT"] = np.ascontiguousarray(np.asarray(inputs["L0_attn_Wr"], dtype=np.float32).T)
    d["woT"] = np.ascontiguousarray(np.asarray(inputs["L0_attn_Wo"], dtype=np.float32).T)
    d["ln1g"] = _vec_tile(inputs["L0_attn_ln1_g"])
    d["ln1b"] = _vec_tile(inputs["L0_attn_ln1_b"])
    d["mixk"] = _vec_tile(inputs["L0_attn_att_mix_k"])
    d["mixv"] = _vec_tile(inputs["L0_attn_att_mix_v"])
    d["mixr"] = _vec_tile(inputs["L0_attn_att_mix_r"])
    d["td"] = _vec_tile(inputs["L0_attn_time_decay"])
    d["tf"] = _vec_tile(inputs["L0_attn_time_first"])
    d["wkfT"] = np.ascontiguousarray(np.asarray(inputs["L0_attn_Wk_ffn"], dtype=np.float32).T)
    d["wvfT"] = np.ascontiguousarray(np.asarray(inputs["L0_attn_Wv_ffn"], dtype=np.float32).T)
    d["wrfT"] = np.ascontiguousarray(np.asarray(inputs["L0_attn_Wr_ffn"], dtype=np.float32).T)
    d["ln2g"] = _vec_tile(inputs["L0_attn_ln2_g"])
    d["ln2b"] = _vec_tile(inputs["L0_attn_ln2_b"])
    d["fmixk"] = _vec_tile(inputs["L0_attn_ffn_mix_k"])
    d["fmixr"] = _vec_tile(inputs["L0_attn_ffn_mix_r"])
    d["qwT"] = np.ascontiguousarray(np.asarray(inputs["L1_attn_Wq"], dtype=np.float32).T)
    d["kwT"] = np.ascontiguousarray(np.asarray(inputs["L1_attn_Wk"], dtype=np.float32).T)
    d["vwT"] = np.ascontiguousarray(np.asarray(inputs["L1_attn_Wv"], dtype=np.float32).T)
    d["owT"] = np.ascontiguousarray(np.asarray(inputs["L1_attn_Wo"], dtype=np.float32).T)
    d["bq"] = _vec_tile(inputs["L1_attn_bq"])
    d["bk"] = _vec_tile(inputs["L1_attn_bk"])
    d["bo"] = _vec_tile(inputs["L1_attn_bo"])
    d["bv_row"] = np.ascontiguousarray(
        np.asarray(inputs["L1_attn_bv"], dtype=np.float32).reshape(1, C))
    rr = np.arange(128)
    ii = (rr % 64) // 2
    tpos = np.arange(T, dtype=np.float64)
    ang = tpos[None, :] / (10000.0 ** (2.0 * ii[:, None] / 64.0))
    d["rcos"] = np.cos(ang).astype(np.float32)
    d["rsin"] = np.sin(ang).astype(np.float32)
    S = np.zeros((128, 128), np.float32)
    for blk in (0, 64):
        for i in range(32):
            S[blk + 2 * i + 1, blk + 2 * i] = -1.0
            S[blk + 2 * i, blk + 2 * i + 1] = 1.0
    d["ropeS"] = S
    cm = np.zeros((128, 4 * W), np.float32)
    kk_ = np.arange(128)
    qq_ = np.arange(W)
    for o in range(4):
        cm[:, o * W:(o + 1) * W] = (128 * o + kk_[:, None]) <= qq_[None, :]
    import ml_dtypes
    d["cmask"] = cm.astype(ml_dtypes.bfloat16)
    d["wgT"] = np.ascontiguousarray(np.asarray(inputs["L1_ffn_w_gate"]).T)
    d["wuT"] = np.ascontiguousarray(np.asarray(inputs["L1_ffn_w_up"]).T)
    d["wdT"] = np.ascontiguousarray(np.asarray(inputs["L1_ffn_w_down"]).T)
    return d


def kernel(**inputs):
    x = np.asarray(inputs["L0_attn_x"])
    shared = _host_inputs(inputs)
    phases = [
        ("l0ffn", "x", "X"),
        ("rwkv_att", "X", "Y"),
        ("rwkv_ffn", "Y", "X", "X"),
        ("l0ffn", "X", "X"),
        ("l1ffn", "X", "X"),
        ("l1attn", "X", "X"),
        ("l1ffn", "X", "out"),
    ]
    nc, used = _build_program(phases)
    in_maps = []
    for core in range(8):
        m = {k: v for k, v in shared.items() if k in used}
        m["x"] = np.ascontiguousarray(x[core].T)
        in_maps.append(m)
    res = run_bass_kernel_spmd(nc, in_maps, core_ids=list(range(8)))
    out = np.stack([np.asarray(r["out"]).T for r in res.results])
    return out.astype(np.float32)


# version 22
# speedup vs baseline: 1.0283x; 1.0283x over previous
import numpy as np
from contextlib import ExitStack
import concourse.bass as bass
import concourse.mybir as mybir
from concourse.bass_utils import run_bass_kernel_spmd

F32 = mybir.dt.float32
BF16 = mybir.dt.bfloat16
I32 = mybir.dt.int32
AF = mybir.ActivationFunctionType
ALU = mybir.AluOpType

T = 4096
C = 1024
NT = C // 128
W = 512
NCH = T // W
EPS = 1e-5

FUSED = True



def ln_stats_chunk_tensor(tensor, ones_bf, xb, xq, ps_sum, ps_sq, w=W):
    for i in range(NT):
        tensor.matmul(out=ps_sum[:, 0:w], lhsT=ones_bf[:, 0:128], rhs=xb[:, i * w:(i + 1) * w],
                      start=(i == 0), stop=(i == NT - 1))
    m = None
    for i in range(NT):
        m = tensor.matmul(out=ps_sq[:, 0:w], lhsT=ones_bf[:, 0:128], rhs=xq[:, i * w:(i + 1) * w],
                          start=(i == 0), stop=(i == NT - 1))
    return m



def p_l0ffn(nc, semf, src, dst, g_ap, b_ap, tag):
    es = ExitStack()
    sem = lambda n: semf(n)
    with es:
        sb = lambda n, s, d: es.enter_context(nc.sbuf_tensor(tag + n, s, d))
        zt = sb("zt", [128, 2 * NT * W], F32)
        xb = sb("xb", [128, 2 * NT * W], BF16)
        xq = sb("xq", [128, 2 * NT * W], BF16)
        u = sb("u", [128, NT * W], F32)
        mu = sb("mu", [128, W], F32)
        t1 = sb("t1", [128, W], F32)
        t2 = sb("t2", [128, W], F32)
        t3 = sb("t3", [128, W], F32)
        ss = sb("ss", [128, W], F32)
        gcol = sb("gcol", [128, NT], F32)
        bcol = sb("bcol", [128, NT], F32)
        gh = sb("gh", [128, NT], F32)
        bh = sb("bh", [128, NT], F32)
        epsv = sb("epsv", [128, 1], F32)
        eps2v = sb("eps2v", [128, 1], F32)
        ones = sb("ones", [128, 128], BF16)
        ps_sum = es.enter_context(nc.psum_tensor(tag + "pss", [128, W], F32))
        ps_sq = es.enter_context(nc.psum_tensor(tag + "psq", [128, W], F32))

        ldp = (sem("ld0"), sem("ld1"))
        pre = sem("pre")
        cast = sem("cast")
        sq = sem("sq")
        mmst = sem("mmst")
        sA = sem("sA")
        vA = sem("vA")
        sB = sem("sB")
        vB = sem("vB")
        sC = sem("sC")
        vD = sem("vD")
        fin = sem("fin")
        st = sem("st")

        def BLK(c, i):
            return slice((c % 2) * NT * W + i * W, (c % 2) * NT * W + (i + 1) * W)

        with nc.Block(tag) as blk:
            @blk.sync
            def _(sync):
                sync.dma_start(out=gcol[:], in_=g_ap).then_inc(pre, 16)
                sync.dma_start(out=bcol[:], in_=b_ap).then_inc(pre, 16)
                src3 = src.rearrange("(i p) t -> p i t", p=128)
                dst3 = dst.rearrange("(i p) t -> p i t", p=128)
                for c in range(NCH):
                    if c >= 2:
                        sync.wait_ge(st, 16 * (c - 1))
                    sync.dma_start(out=zt[:, (c % 2) * NT * W:(c % 2 + 1) * NT * W].rearrange(
                                       "p (i t) -> p i t", i=NT),
                                   in_=src3[:, :, c * W:(c + 1) * W]).then_inc(ldp[c % 2], 16)
                    sync.wait_ge(fin, NT * (c + 1))
                    sync.dma_start(out=dst3[:, :, c * W:(c + 1) * W],
                                   in_=zt[:, (c % 2) * NT * W:(c % 2 + 1) * NT * W].rearrange(
                                       "p (i t) -> p i t", i=NT)).then_inc(st, 16)
                sync.wait_ge(st, 16 * NCH)

            @blk.scalar
            def _(scalar):
                for c in range(NCH):
                    scalar.wait_ge(ldp[c % 2], 16 * (c // 2 + 1))
                    if c >= 2:
                        scalar.wait_ge(mmst, c - 1)
                    for i in range(NT):
                        ins = scalar.copy(out=xb[:, BLK(c, i)], in_=zt[:, BLK(c, i)])
                    ins.then_inc(cast, 1)
                    scalar.wait_ge(mmst, c + 1)
                    scalar.activation(out=mu[:], in_=ps_sum[:, 0:W], func=AF.Copy, scale=1.0 / C)
                    scalar.activation(out=t1[:], in_=ps_sq[:, 0:W], func=AF.Copy,
                                      scale=1.0 / C).then_inc(sA, 1)
                    scalar.wait_ge(vA, c + 1)
                    scalar.activation(out=t2[:], in_=t1[:], func=AF.Ln, bias=eps2v[:, 0:1],
                                      scale=float(1.0 + EPS))
                    scalar.activation(out=ss[:], in_=t2[:], func=AF.Exp, scale=-0.5).then_inc(sC, 1)

            @blk.vector
            def _(vector):
                vector.memset(ones[:], 1.0)
                vector.memset(epsv[:], float(EPS))
                vector.memset(eps2v[:], float(EPS * EPS))
                vector.wait_ge(pre, 32)
                vector.tensor_scalar(out=gh[:], in0=gcol[:], scalar1=0.5, scalar2=None, op0=ALU.mult)
                vector.tensor_scalar(out=bh[:], in0=bcol[:], scalar1=0.5, scalar2=None,
                                     op0=ALU.mult).then_inc(pre, 1)
                for c in range(NCH):
                    vector.wait_ge(cast, c + 1)
                    for i in range(NT):
                        ins = vector.tensor_tensor(out=xq[:, BLK(c, i)], in0=xb[:, BLK(c, i)],
                                                   in1=xb[:, BLK(c, i)], op=ALU.mult)
                    ins.then_inc(sq, 1)
                    vector.wait_ge(sA, c + 1)
                    vector.tensor_tensor(out=t2[:], in0=mu[:], in1=mu[:], op=ALU.mult)
                    vector.tensor_tensor(out=t1[:], in0=t1[:], in1=t2[:],
                                         op=ALU.subtract).then_inc(vA, 1)
                    vector.wait_ge(sC, c + 1)
                    for i in range(NT):
                        sl = BLK(c, i)
                        usl = slice(i * W, (i + 1) * W)
                        vector.tensor_tensor(out=u[:, usl], in0=zt[:, sl], in1=mu[:], op=ALU.subtract)
                        vector.tensor_tensor(out=u[:, usl], in0=u[:, usl], in1=ss[:], op=ALU.mult)
                        vector.scalar_tensor_tensor(out=zt[:, sl], in0=u[:, usl],
                                                    scalar=gh[:, i:i + 1], in1=zt[:, sl],
                                                    op0=ALU.mult, op1=ALU.add)
                        vector.tensor_scalar(out=zt[:, sl], in0=zt[:, sl], scalar1=bh[:, i:i + 1],
                                             scalar2=None, op0=ALU.add).then_inc(fin, 1)

            @blk.gpsimd
            def _(gpsimd):
                pass

            @blk.tensor
            def _(tensor):
                tensor.wait_ge(pre, 33)
                for c in range(NCH):
                    tensor.wait_ge(cast, c + 1)
                    tensor.wait_ge(sq, c + 1)
                    if c > 0:
                        tensor.wait_ge(sA, c)
                    for i in range(NT):
                        tensor.matmul(out=ps_sum[:, 0:W], lhsT=ones[:, 0:128], rhs=xb[:, BLK(c, i)],
                                      start=(i == 0), stop=(i == NT - 1))
                    for i in range(NT):
                        m = tensor.matmul(out=ps_sq[:, 0:W], lhsT=ones[:, 0:128], rhs=xq[:, BLK(c, i)],
                                          start=(i == 0), stop=(i == NT - 1))
                    m.then_inc(mmst, 1)



HID = 3072
NH = HID // 128

def p_l1ffn(nc, semf, src, dst, wg_ap, wu_ap, wd_ap, wd_scr, wu_scr, tag):
    es = ExitStack()
    sem = lambda n: semf(n)
    with es:
        sb = lambda n, s, d: es.enter_context(nc.sbuf_tensor(tag + n, s, d))
        zt = sb("zt", [128, 2 * NT * W], F32)
        xb = sb("xb", [128, 2 * NT * W], BF16)
        xq = sb("xq", [128, 2 * NT * W], BF16)
        hbuf = sb("hbuf", [128, NH * W], BF16)
        sg = sb("sg", [128, W], BF16)
        mu = sb("mu", [128, W], F32)
        t1 = sb("t1", [128, W], F32)
        t2 = sb("t2", [128, W], F32)
        t3 = sb("t3", [128, W], F32)
        epsv = sb("epsv", [128, 1], F32)
        ones = sb("ones", [128, 128], BF16)
        wgi = sb("wgi", [128, HID], I32)
        wdq = sb("wdq", [128, C], BF16)
        wg_sb = sb("wg_sb", [128, NT * HID], BF16)
        wuq = sb("wuq", [128, HID], BF16)
        wustr = sb("wustr", [128, 4 * NT * 128], BF16)
        wdst = sb("wdst", [128, 4 * NH * 128], BF16)
        ps_sum = es.enter_context(nc.psum_tensor(tag + "pss", [128, W], F32))
        ps_sq = es.enter_context(nc.psum_tensor(tag + "psq", [128, W], F32))
        ps_g = es.enter_context(nc.psum_tensor(tag + "psg", [128, 2 * W], F32))
        ps_u = es.enter_context(nc.psum_tensor(tag + "psu", [128, 2 * W], F32))
        ps_d = es.enter_context(nc.psum_tensor(tag + "psd", [128, 2 * W], F32))

        ldp = (sem("ld0"), sem("ld1"))
        wldp = (sem("wld0"), sem("wld1"))
        pre = sem("pre")
        wdsp = (sem("wds0"), sem("wds1"))
        wdlp = (sem("wdl0"), sem("wdl1"), sem("wdl2"), sem("wdl3"))
        wusp = (sem("wus0"), sem("wus1"))
        wulp = (sem("wul0"), sem("wul1"), sem("wul2"), sem("wul3"))
        cast = sem("cast")
        sq = sem("sq")
        mmst = sem("mmst")
        sA = sem("sA")
        vA = sem("vA")
        sB = sem("sB")
        xn_s = sem("xn")
        mmg = sem("mmg")
        mmu = sem("mmu")
        mmd = sem("mmd")
        sil = sem("sil")
        hbv = sem("hbv")
        fin = sem("fin")
        st = sem("st")

        def PAR(c):
            return c % 2

        def S(c, base_w=W):
            return slice(0, base_w)

        def BLK(c, i, w=W):
            off = PAR(c) * NT * w
            return slice(off + i * w, off + (i + 1) * w)

        with nc.Block(tag) as blk:
            @blk.sync
            def _(sync):
                for kk in range(16):
                    if kk >= 1:
                        sync.wait_ge(pre, kk)
                    ap = wg_ap if kk < 8 else wu_ap
                    i = kk % 8
                    sync.dma_start(out=wgi[:, 0:HID],
                                   in_=ap[i * 128:(i + 1) * 128, :]).then_inc(wldp[kk % 2], 16)
                    if kk >= 8:
                        sync.wait_ge(pre, kk + 1)
                        sync.dma_start(out=wu_scr[i * 128:(i + 1) * 128, :],
                                       in_=wuq[:, 0:HID]).then_inc(wusp[i % 2], 16)
                for j in range(NH):
                    kk = 16 + j
                    sync.wait_ge(pre, kk)
                    sync.dma_start(out=wgi[:, 0:C],
                                   in_=wd_ap[j * 128:(j + 1) * 128, :]).then_inc(wldp[kk % 2], 16)
                    sync.wait_ge(pre, kk + 1)
                    sync.dma_start(out=wd_scr[j * 128:(j + 1) * 128, :],
                                   in_=wdq[:, 0:C]).then_inc(wdsp[j % 2], 16)
                src3 = src.rearrange("(i p) t -> p i t", p=128)
                dst3 = dst.rearrange("(i p) t -> p i t", p=128)
                wd3 = wd_scr.rearrange("(kh p) c -> p kh c", p=128)
                wu3 = wu_scr.rearrange("(ct p) h -> p ct h", p=128)
                def emit_load(cc):
                    if cc >= 2:
                        sync.wait_ge(st, 16 * (cc - 1))
                    sync.dma_start(out=zt[:, PAR(cc) * NT * W:(PAR(cc) + 1) * NT * W].rearrange(
                                       "p (i t) -> p i t", i=NT),
                                   in_=src3[:, :, cc * W:(cc + 1) * W]).then_inc(ldp[cc % 2], 16)

                emit_load(0)
                if NCH > 1:
                    emit_load(1)
                for c in range(NCH):
                    for h in range(NH):
                        gp = c * NH + h
                        if gp >= 4:
                            sync.wait_ge(mmu, gp - 3)
                        if gp == 0:
                            sync.wait_ge(wusp[0], 16 * 4)
                            sync.wait_ge(wusp[1], 16 * 4)
                        sync.dma_start(
                            out=wustr[:, (gp % 4) * NT * 128:((gp % 4) + 1) * NT * 128].rearrange(
                                "p (ct c) -> p ct c", ct=NT),
                            in_=wu3[:, :, h * 128:(h + 1) * 128]).then_inc(wulp[gp % 4], 16)
                    for d in range(NT):
                        g = c * NT + d
                        if g >= 4:
                            sync.wait_ge(mmd, g - 3)
                        if g == 0:
                            sync.wait_ge(wdsp[0], 16 * (NH // 2))
                            sync.wait_ge(wdsp[1], 16 * (NH // 2))
                        sync.dma_start(
                            out=wdst[:, (g % 4) * NH * 128:((g % 4) + 1) * NH * 128].rearrange(
                                "p (kh c) -> p kh c", kh=NH),
                            in_=wd3[:, :, d * 128:(d + 1) * 128]).then_inc(wdlp[g % 4], 16)
                    sync.wait_ge(fin, NT * (c + 1))
                    sync.dma_start(out=dst3[:, :, c * W:(c + 1) * W],
                                   in_=zt[:, PAR(c) * NT * W:(PAR(c) + 1) * NT * W].rearrange(
                                       "p (i t) -> p i t", i=NT)).then_inc(st, 16)
                    if c + 2 < NCH:
                        emit_load(c + 2)
                sync.wait_ge(st, 16 * NCH)

            @blk.gpsimd
            def _(gpsimd):
                pass

            @blk.vector
            def _(vector):
                vector.memset(ones[:], 1.0)
                vector.memset(epsv[:], float(EPS))
                def emit_prologue(cc):
                    vector.wait_ge(cast, cc + 1)
                    for i in range(NT):
                        ins = vector.tensor_tensor(out=xq[:, BLK(cc, i)], in0=xb[:, BLK(cc, i)],
                                                   in1=xb[:, BLK(cc, i)], op=ALU.mult)
                    ins.then_inc(sq, 1)
                    vector.wait_ge(sA, cc + 1)
                    vector.tensor_tensor(out=t2[:, S(cc)], in0=mu[:, S(cc)], in1=mu[:, S(cc)],
                                         op=ALU.mult)
                    vector.tensor_tensor(out=t1[:, S(cc)], in0=t1[:, S(cc)], in1=t2[:, S(cc)],
                                         op=ALU.subtract).then_inc(vA, 1)
                    vector.wait_ge(sB, cc + 1)
                    vector.wait_ge(mmst, cc + 1)
                    for i in range(NT):
                        vector.tensor_tensor(out=t2[:, S(cc)], in0=zt[:, BLK(cc, i)], in1=mu[:, S(cc)],
                                             op=ALU.subtract)
                        ins = vector.tensor_tensor(out=xq[:, BLK(cc, i)], in0=t2[:, S(cc)],
                                                   in1=t3[:, S(cc)], op=ALU.mult)
                    ins.then_inc(xn_s, 1)

                emit_prologue(0)
                for c in range(NCH):
                    for h in range(NH):
                        vector.wait_ge(sil, c * NH + h + 1)
                        vector.wait_ge(mmu, c * NH + h + 1)
                        vector.tensor_tensor(out=hbuf[:, h * W:(h + 1) * W],
                                             in0=ps_u[:, (h % 2) * W:(h % 2 + 1) * W],
                                             in1=sg[:, 0:W],
                                             op=ALU.mult).then_inc(hbv, 1)
                    if c + 1 < NCH:
                        emit_prologue(c + 1)
                    for d in range(NT):
                        vector.wait_ge(mmd, c * NT + d + 1)
                        vector.scalar_tensor_tensor(out=zt[:, BLK(c, d)],
                                                    in0=ps_d[:, (d % 2) * W:(d % 2 + 1) * W],
                                                    scalar=0.5, in1=zt[:, BLK(c, d)],
                                                    op0=ALU.mult, op1=ALU.add).then_inc(fin, 1)

            @blk.scalar
            def _(scalar):
                for kk in range(16):
                    i = kk % 8
                    scalar.wait_ge(wldp[kk % 2], 16 * (kk // 2 + 1))
                    if kk < 8:
                        scalar.activation(out=wg_sb[:, i * HID:(i + 1) * HID], in_=wgi[:, 0:HID],
                                          func=AF.Copy, scale=1.0 / 127.0).then_inc(pre, 1)
                    else:
                        if kk >= 9:
                            scalar.wait_ge(wusp[(i - 1) % 2], 16 * ((i - 1) // 2 + 1))
                        scalar.activation(out=wuq[:, 0:HID], in_=wgi[:, 0:HID],
                                          func=AF.Copy, scale=1.0 / 127.0).then_inc(pre, 1)
                for j in range(NH):
                    kk = 16 + j
                    scalar.wait_ge(wldp[kk % 2], 16 * (kk // 2 + 1))
                    if j >= 1:
                        scalar.wait_ge(wdsp[(j - 1) % 2], 16 * ((j - 1) // 2 + 1))
                    scalar.activation(out=wdq[:, 0:C], in_=wgi[:, 0:C],
                                      func=AF.Copy, scale=1.0 / 127.0).then_inc(pre, 1)
                for c in range(NCH):
                    scalar.wait_ge(ldp[c % 2], 16 * (c // 2 + 1))
                    if c >= 2:
                        scalar.wait_ge(mmst, c - 1)
                    for i in range(NT):
                        ins = scalar.copy(out=xb[:, BLK(c, i)], in_=zt[:, BLK(c, i)])
                    ins.then_inc(cast, 1)
                    scalar.wait_ge(mmst, c + 1)
                    scalar.activation(out=mu[:, S(c)], in_=ps_sum[:, 0:W], func=AF.Copy, scale=1.0 / C)
                    scalar.activation(out=t1[:, S(c)], in_=ps_sq[:, 0:W], func=AF.Copy,
                                      scale=1.0 / C).then_inc(sA, 1)
                    scalar.wait_ge(vA, c + 1)
                    scalar.activation(out=t2[:, S(c)], in_=t1[:, S(c)], func=AF.Ln, bias=epsv[:, 0:1])
                    scalar.activation(out=t3[:, S(c)], in_=t2[:, S(c)], func=AF.Exp,
                                      scale=-0.5).then_inc(sB, 1)
                    for h in range(NH):
                        scalar.wait_ge(mmg, c * NH + h + 1)
                        if c * NH + h >= 1:
                            scalar.wait_ge(hbv, c * NH + h)
                        scalar.activation(out=sg[:, 0:W],
                                          in_=ps_g[:, (h % 2) * W:(h % 2 + 1) * W],
                                          func=AF.Silu).then_inc(sil, 1)

            @blk.tensor
            def _(tensor):
                tensor.wait_ge(pre, 16)

                def emit_stats(cc):
                    tensor.wait_ge(cast, cc + 1)
                    tensor.wait_ge(sq, cc + 1)
                    if cc > 0:
                        tensor.wait_ge(sA, cc)
                    for i in range(NT):
                        tensor.matmul(out=ps_sum[:, 0:W], lhsT=ones[:, 0:128], rhs=xb[:, BLK(cc, i)],
                                      start=(i == 0), stop=(i == NT - 1))
                    for i in range(NT):
                        m = tensor.matmul(out=ps_sq[:, 0:W], lhsT=ones[:, 0:128], rhs=xq[:, BLK(cc, i)],
                                          start=(i == 0), stop=(i == NT - 1))
                    m.then_inc(mmst, 1)

                emit_stats(0)
                for c in range(NCH):
                    tensor.wait_ge(xn_s, c + 1)
                    for h in range(NH):
                        if c * NH + h >= 2:
                            tensor.wait_ge(sil, c * NH + h - 1)
                            tensor.wait_ge(hbv, c * NH + h - 1)
                        for ct in range(NT):
                            m = tensor.matmul(out=ps_g[:, (h % 2) * W:(h % 2 + 1) * W],
                                              lhsT=wg_sb[:, ct * HID + h * 128: ct * HID + (h + 1) * 128],
                                              rhs=xq[:, BLK(c, ct)],
                                              start=(ct == 0), stop=(ct == NT - 1))
                        m.then_inc(mmg, 1)
                        gp = c * NH + h
                        tensor.wait_ge(wulp[gp % 4], 16 * (gp // 4 + 1))
                        for ct in range(NT):
                            m = tensor.matmul(out=ps_u[:, (h % 2) * W:(h % 2 + 1) * W],
                                              lhsT=wustr[:, ((gp % 4) * NT + ct) * 128:((gp % 4) * NT + ct + 1) * 128],
                                              rhs=xq[:, BLK(c, ct)],
                                              start=(ct == 0), stop=(ct == NT - 1))
                        m.then_inc(mmu, 1)
                    if c + 1 < NCH:
                        emit_stats(c + 1)
                    for d in range(NT):
                        g = c * NT + d
                        if g >= 2:
                            tensor.wait_ge(fin, g - 1)
                        tensor.wait_ge(wdlp[g % 4], 16 * (g // 4 + 1))
                        for kh in range(NH):
                            tensor.wait_ge(hbv, c * NH + kh + 1)
                            m = tensor.matmul(out=ps_d[:, (d % 2) * W:(d % 2 + 1) * W],
                                              lhsT=wdst[:, ((g % 4) * NH + kh) * 128:((g % 4) * NH + kh + 1) * 128],
                                              rhs=hbuf[:, kh * W:(kh + 1) * W],
                                              start=(kh == 0), stop=(kh == NH - 1))
                        m.then_inc(mmd, 1)



WR = 256
NCHR = T // WR

def p_rwkv_att(nc, semf, src, dst, prm, wo_scr, tag):
    WA = 512
    NCA = T // WA
    es = ExitStack()
    sem = lambda n: semf(n)
    with es:
        sb = lambda n, s, d: es.enter_context(nc.sbuf_tensor(tag + n, s, d))
        zt = sb("zt", [128, 2 * NT * WA], F32)
        xb = sb("xb", [128, NT * WA], BF16)
        xq = sb("xq", [128, NT * WA], BF16)
        wmx = sb("wmx", [128, NT * (WA + 1)], BF16)
        Et = sb("Et", [128, NT * WA], F32)
        Gt = sb("Gt", [128, NT * WA], F32)
        At = sb("At", [128, NT * WA], F32)
        Bt = sb("Bt", [128, NT * WA], F32)
        srt = sb("srt", [128, NT * WA], BF16)
        rwb = sb("rwb", [128, NT * WA], BF16)
        mu = sb("mu", [128, WA], F32)
        t1 = sb("t1", [128, WA], F32)
        t2 = sb("t2", [128, WA], F32)
        t3 = sb("t3", [128, WA], F32)
        ss = sb("ss", [128, WA], F32)
        epsv = sb("epsv", [128, 1], F32)
        eps2v = sb("eps2v", [128, 1], F32)
        ones = sb("ones", [128, 128], BF16)
        prmt = sb("prmt", [128, 7 * NT], F32)
        eu = sb("eu", [128, NT], F32)
        lcol = sb("lcol", [128, NT], F32)
        omk = sb("omk", [128, 3 * NT], F32)
        cA = sb("cA", [128, NT], F32)
        cB = sb("cB", [128, NT], F32)
        wstg = sb("wstg", [128, C], F32)
        wk_sb = sb("wk_sb", [128, NT * C], BF16)
        wv_sb = sb("wv_sb", [128, NT * C], BF16)
        wr_sb = sb("wr_sb", [128, NT * C], BF16)
        wob = sb("wob", [128, C], BF16)
        wost = sb("wost", [128, 2 * NT * 128], BF16)
        ps_sum = es.enter_context(nc.psum_tensor(tag + "pss", [128, WA], F32))
        ps_sq = es.enter_context(nc.psum_tensor(tag + "psq", [128, WA], F32))
        ps_p0 = es.enter_context(nc.psum_tensor(tag + "pp0", [128, WA], F32))
        ps_p1 = es.enter_context(nc.psum_tensor(tag + "pp1", [128, WA], F32))
        ps_o0 = es.enter_context(nc.psum_tensor(tag + "po0", [128, WA], F32))
        ps_o1 = es.enter_context(nc.psum_tensor(tag + "po1", [128, WA], F32))
        ps_p = (ps_p0, ps_p1)
        ps_o = (ps_o0, ps_o1)

        ldp = (sem("ld0"), sem("ld1")); prs = sem("prs"); wcv = sem("wcv")
        wldp = (sem("wld0"), sem("wld1"))
        wosp = (sem("wos0"), sem("wos1"))
        wolp = (sem("wol0"), sem("wol1"))
        sprep = sem("sprep"); vprep = sem("vprep")
        cast = sem("cast"); sq = sem("sq"); mmst = sem("mmst")
        sA = sem("sA"); vA = sem("vA"); sB = sem("sB"); vB = sem("vB"); sC = sem("sC")
        xkr = sem("xkr"); xvr = sem("xvr"); xrr = sem("xrr")
        mmk = sem("mmk"); mmv = sem("mmv"); mmr = sem("mmr"); mmo = sem("mmo")
        ek = sem("ek"); gv = sem("gv"); rk = sem("rk"); rv = sem("rv")
        fin = sem("fin"); st = sem("st")

        def ZB(c, i):
            off = (c % 2) * NT * WA
            return slice(off + i * WA, off + (i + 1) * WA)

        w_aps = [prm["wkT"], prm["wvT"], prm["wrT"], prm["woT"]]
        w_sbs = [wk_sb, wv_sb, wr_sb, None]

        with nc.Block(tag) as blk:
            @blk.sync
            def _(sync):
                for j, p in enumerate([prm["ln1g"], prm["ln1b"], prm["mixk"], prm["mixv"],
                                       prm["mixr"], prm["td"], prm["tf"]]):
                    sync.dma_start(out=prmt[:, j * NT:(j + 1) * NT], in_=p).then_inc(prs, 16)
                for kk in range(32):
                    if kk >= 1:
                        sync.wait_ge(wcv, kk)
                    w_ap = w_aps[kk // 8]
                    i = kk % 8
                    sync.dma_start(out=wstg[:, 0:C],
                                   in_=w_ap[i * 128:(i + 1) * 128, :]).then_inc(wldp[kk % 2], 16)
                    if kk >= 24:
                        sync.wait_ge(wcv, kk + 1)
                        sync.dma_start(out=wo_scr[i * 128:(i + 1) * 128, :],
                                       in_=wob[:, 0:C]).then_inc(wosp[i % 2], 16)
                src3 = src.rearrange("(i p) t -> p i t", p=128)
                dst3 = dst.rearrange("(i p) t -> p i t", p=128)
                wo3 = wo_scr.rearrange("(dt p) c -> p dt c", p=128)

                def emit_load(cc):
                    if cc >= 2:
                        sync.wait_ge(st, 16 * (cc - 1))
                    sync.dma_start(out=zt[:, (cc % 2) * NT * WA:(cc % 2 + 1) * NT * WA].rearrange(
                                       "p (i t) -> p i t", i=NT),
                                   in_=src3[:, :, cc * WA:(cc + 1) * WA]).then_inc(ldp[cc % 2], 16)

                emit_load(0)
                if NCA > 1:
                    emit_load(1)
                for c in range(NCA):
                    for jo in range(NT):
                        g = c * NT + jo
                        if g >= 2:
                            sync.wait_ge(mmo, g - 1)
                        if g == 0:
                            sync.wait_ge(wosp[0], 16 * 4)
                            sync.wait_ge(wosp[1], 16 * 4)
                        sync.dma_start(
                            out=wost[:, (g % 2) * NT * 128:((g % 2) + 1) * NT * 128].rearrange(
                                "p (dt c) -> p dt c", dt=NT),
                            in_=wo3[:, :, jo * 128:(jo + 1) * 128]).then_inc(wolp[g % 2], 16)
                    sync.wait_ge(fin, NT * (c + 1))
                    zt3c = zt[:, (c % 2) * NT * WA:(c % 2 + 1) * NT * WA].rearrange(
                        "p (i t) -> p i t", i=NT)
                    sync.dma_start(out=dst3[:, :, c * WA:(c + 1) * WA], in_=zt3c).then_inc(st, 16)
                    if c + 2 < NCA:
                        emit_load(c + 2)
                sync.wait_ge(st, 16 * NCA)

            @blk.scalar
            def _(scalar):
                scalar.wait_ge(prs, 112)
                scalar.activation(out=eu[:], in_=prmt[:, 6 * NT:7 * NT], func=AF.Exp)
                scalar.activation(out=lcol[:], in_=prmt[:, 5 * NT:6 * NT], func=AF.Exp)
                scalar.activation(out=lcol[:], in_=lcol[:], func=AF.Exp,
                                  scale=-1.0).then_inc(sprep, 1)
                for kk in range(32):
                    scalar.wait_ge(wldp[kk % 2], 16 * (kk // 2 + 1))
                    i = kk % 8
                    if kk < 24:
                        scalar.copy(out=w_sbs[kk // 8][:, i * C:(i + 1) * C],
                                    in_=wstg[:, 0:C]).then_inc(wcv, 1)
                    else:
                        if kk >= 25:
                            scalar.wait_ge(wosp[(i - 1) % 2], 16 * ((i - 1) // 2 + 1))
                        scalar.copy(out=wob[:, 0:C], in_=wstg[:, 0:C]).then_inc(wcv, 1)
                for c in range(NCA):
                    scalar.wait_ge(ldp[c % 2], 16 * (c // 2 + 1))
                    for i in range(NT):
                        ins = scalar.copy(out=xb[:, i * WA:(i + 1) * WA], in_=zt[:, ZB(c, i)])
                    ins.then_inc(cast, 1)
                    if c > 0:
                        scalar.wait_ge(mmr, NT * c)
                    for i in range(NT):
                        ins = scalar.activation(out=xq[:, i * WA:(i + 1) * WA],
                                                in_=xb[:, i * WA:(i + 1) * WA], func=AF.Square)
                    ins.then_inc(sq, 1)
                    scalar.wait_ge(mmst, c + 1)
                    scalar.activation(out=mu[:], in_=ps_sum[:, 0:WA], func=AF.Copy, scale=1.0 / C)
                    scalar.activation(out=t1[:], in_=ps_sq[:, 0:WA], func=AF.Copy,
                                      scale=1.0 / C).then_inc(sA, 1)
                    scalar.wait_ge(vA, c + 1)
                    scalar.activation(out=t2[:], in_=t1[:], func=AF.Ln, bias=epsv[:, 0:1])
                    scalar.activation(out=t3[:], in_=t2[:], func=AF.Exp, scale=-0.5)
                    scalar.activation(out=t2[:], in_=t1[:], func=AF.Ln, bias=eps2v[:, 0:1],
                                      scale=float(1.0 + EPS))
                    scalar.activation(out=ss[:], in_=t2[:], func=AF.Exp, scale=-0.5).then_inc(sC, 1)
                    for i in range(NT):
                        scalar.wait_ge(mmk, NT * c + i + 1)
                        scalar.activation(out=Et[:, i * WA:(i + 1) * WA],
                                          in_=ps_p[i % 2][:, 0:WA], func=AF.Exp).then_inc(ek, 1)
                    for i in range(NT):
                        scalar.wait_ge(mmr, NT * c + i + 1)
                        scalar.activation(out=srt[:, i * WA:(i + 1) * WA],
                                          in_=ps_p[i % 2][:, 0:WA], func=AF.Sigmoid).then_inc(rk, 1)

            @blk.vector
            def _(vector):
                mixbuf = xq
                vector.memset(ones[:], 1.0)
                vector.memset(epsv[:], float(EPS))
                vector.memset(eps2v[:], float(EPS * EPS))
                vector.memset(cA[:], 0.0)
                vector.memset(cB[:], 0.0)
                for i in range(NT):
                    vector.memset(wmx[:, i * (WA + 1):i * (WA + 1) + 1], 0.0)
                vector.wait_ge(prs, 112)
                for j in range(3):
                    vector.tensor_scalar(out=omk[:, j * NT:(j + 1) * NT],
                                         in0=prmt[:, (2 + j) * NT:(3 + j) * NT],
                                         scalar1=-1.0, scalar2=1.0, op0=ALU.mult, op1=ALU.add)
                vector.wait_ge(sprep, 1)
                vector.engine_nop().then_inc(vprep, NT)
                gcol = prmt[:, 0:NT]
                bcol = prmt[:, NT:2 * NT]
                def emit_prologue(cc):
                    vector.wait_ge(sA, cc + 1)
                    vector.tensor_tensor(out=t2[:], in0=mu[:], in1=mu[:], op=ALU.mult)
                    vector.tensor_tensor(out=t1[:], in0=t1[:], in1=t2[:],
                                         op=ALU.subtract).then_inc(vA, 1)
                    vector.wait_ge(sC, cc + 1)
                    vector.wait_ge(mmst, cc + 1)
                    for i in range(NT):
                        vector.tensor_tensor(out=t2[:], in0=zt[:, ZB(cc, i)], in1=mu[:],
                                             op=ALU.subtract)
                        vector.tensor_tensor(out=t2[:], in0=t2[:], in1=ss[:], op=ALU.mult)
                        vector.tensor_scalar(out=wmx[:, i * (WA + 1) + 1:(i + 1) * (WA + 1)],
                                             in0=t2[:], scalar1=gcol[:, i:i + 1],
                                             scalar2=bcol[:, i:i + 1], op0=ALU.mult, op1=ALU.add)
                    for i in range(NT):
                        wcur = wmx[:, i * (WA + 1) + 1:(i + 1) * (WA + 1)]
                        wsh = wmx[:, i * (WA + 1):(i + 1) * (WA + 1) - 1]
                        vector.tensor_scalar(out=mixbuf[:, i * WA:(i + 1) * WA], in0=wsh,
                                             scalar1=omk[:, 0 * NT + i:0 * NT + i + 1],
                                             scalar2=None, op0=ALU.mult)
                        ins = vector.scalar_tensor_tensor(out=mixbuf[:, i * WA:(i + 1) * WA],
                                                          in0=wcur,
                                                          scalar=prmt[:, 2 * NT + i:2 * NT + i + 1],
                                                          in1=mixbuf[:, i * WA:(i + 1) * WA],
                                                          op0=ALU.mult, op1=ALU.add)
                    ins.then_inc(xkr, 1)

                for c in range(NCA):
                    emit_prologue(c)
                    vector.wait_ge(mmk, NT * (c + 1))
                    for i in range(NT):
                        wcur = wmx[:, i * (WA + 1) + 1:(i + 1) * (WA + 1)]
                        wsh = wmx[:, i * (WA + 1):(i + 1) * (WA + 1) - 1]
                        vector.tensor_scalar(out=mixbuf[:, i * WA:(i + 1) * WA], in0=wsh,
                                             scalar1=omk[:, 1 * NT + i:1 * NT + i + 1],
                                             scalar2=None, op0=ALU.mult)
                        ins = vector.scalar_tensor_tensor(out=mixbuf[:, i * WA:(i + 1) * WA],
                                                          in0=wcur,
                                                          scalar=prmt[:, 3 * NT + i:3 * NT + i + 1],
                                                          in1=mixbuf[:, i * WA:(i + 1) * WA],
                                                          op0=ALU.mult, op1=ALU.add)
                    ins.then_inc(xvr, 1)
                    for i in range(NT):
                        vector.wait_ge(mmv, NT * c + i + 1)
                        vector.wait_ge(ek, NT * c + i + 1)
                        vector.tensor_tensor(out=Gt[:, i * WA:(i + 1) * WA],
                                             in0=ps_p[i % 2][:, 0:WA],
                                             in1=Et[:, i * WA:(i + 1) * WA], op=ALU.mult).then_inc(gv, 1)
                    for i in range(NT):
                        wcur = wmx[:, i * (WA + 1) + 1:(i + 1) * (WA + 1)]
                        wsh = wmx[:, i * (WA + 1):(i + 1) * (WA + 1) - 1]
                        vector.tensor_scalar(out=mixbuf[:, i * WA:(i + 1) * WA], in0=wsh,
                                             scalar1=omk[:, 2 * NT + i:2 * NT + i + 1],
                                             scalar2=None, op0=ALU.mult)
                        ins = vector.scalar_tensor_tensor(out=mixbuf[:, i * WA:(i + 1) * WA],
                                                          in0=wcur,
                                                          scalar=prmt[:, 4 * NT + i:4 * NT + i + 1],
                                                          in1=mixbuf[:, i * WA:(i + 1) * WA],
                                                          op0=ALU.mult, op1=ALU.add)
                    ins.then_inc(xrr, 1)
                    for i in range(NT):
                        vector.tensor_copy(out=wmx[:, i * (WA + 1):i * (WA + 1) + 1],
                                           in_=wmx[:, (i + 1) * (WA + 1) - 1:(i + 1) * (WA + 1)])
                    for i in range(NT):
                        bsl = slice(i * WA, (i + 1) * WA)
                        lb = lcol[:, i:i + 1].broadcast_to((128, WA))
                        vector.tensor_tensor_scan(out=At[:, bsl], data0=lb, data1=Gt[:, bsl],
                                                  initial=cA[:, i:i + 1], op0=ALU.mult, op1=ALU.add)
                        vector.tensor_tensor_scan(out=Bt[:, bsl], data0=lb, data1=Et[:, bsl],
                                                  initial=cB[:, i:i + 1], op0=ALU.mult, op1=ALU.add)
                        vector.scalar_tensor_tensor(out=Gt[:, i * WA:i * WA + 1],
                                                    in0=Gt[:, i * WA:i * WA + 1],
                                                    scalar=eu[:, i:i + 1], in1=cA[:, i:i + 1],
                                                    op0=ALU.mult, op1=ALU.add)
                        vector.scalar_tensor_tensor(out=Gt[:, i * WA + 1:(i + 1) * WA],
                                                    in0=Gt[:, i * WA + 1:(i + 1) * WA],
                                                    scalar=eu[:, i:i + 1],
                                                    in1=At[:, i * WA:(i + 1) * WA - 1],
                                                    op0=ALU.mult, op1=ALU.add)
                        vector.scalar_tensor_tensor(out=Et[:, i * WA:i * WA + 1],
                                                    in0=Et[:, i * WA:i * WA + 1],
                                                    scalar=eu[:, i:i + 1], in1=cB[:, i:i + 1],
                                                    op0=ALU.mult, op1=ALU.add)
                        vector.scalar_tensor_tensor(out=Et[:, i * WA + 1:(i + 1) * WA],
                                                    in0=Et[:, i * WA + 1:(i + 1) * WA],
                                                    scalar=eu[:, i:i + 1],
                                                    in1=Bt[:, i * WA:(i + 1) * WA - 1],
                                                    op0=ALU.mult, op1=ALU.add)
                        vector.tensor_copy(out=cA[:, i:i + 1], in_=At[:, (i + 1) * WA - 1:(i + 1) * WA])
                        vector.tensor_copy(out=cB[:, i:i + 1], in_=Bt[:, (i + 1) * WA - 1:(i + 1) * WA])
                        vector.reciprocal(out=Bt[:, bsl], in_=Et[:, bsl])
                        vector.tensor_tensor(out=At[:, bsl], in0=Gt[:, bsl], in1=Bt[:, bsl], op=ALU.mult)
                        vector.wait_ge(rk, NT * c + i + 1)
                        ins = vector.tensor_tensor(out=rwb[:, bsl], in0=At[:, bsl], in1=srt[:, bsl],
                                                   op=ALU.mult)
                    ins.then_inc(rv, 1)
                    for j in range(NT):
                        vector.wait_ge(mmo, NT * c + j + 1)
                        sl = ZB(c, j)
                        vector.tensor_tensor(out=t2[:], in0=zt[:, sl], in1=mu[:], op=ALU.subtract)
                        vector.tensor_tensor(out=t2[:], in0=t2[:], in1=t3[:], op=ALU.mult)
                        vector.tensor_tensor(out=zt[:, sl], in0=t2[:], in1=ps_o[j % 2][:, 0:WA],
                                             op=ALU.add).then_inc(fin, 1)

            @blk.tensor
            def _(tensor):
                tensor.wait_ge(wcv, 32)
                tensor.wait_ge(vprep, NT)

                for c in range(NCA):
                    tensor.wait_ge(cast, c + 1)
                    tensor.wait_ge(sq, c + 1)
                    if c > 0:
                        tensor.wait_ge(sA, c)
                    m = ln_stats_chunk_tensor(tensor, ones, xb, xq, ps_sum, ps_sq, w=WA)
                    m.then_inc(mmst, 1)
                    tensor.wait_ge(xkr, c + 1)
                    for i in range(NT):
                        if i >= 2:
                            tensor.wait_ge(ek, NT * c + i - 1)
                        elif c > 0:
                            tensor.wait_ge(rk, NT * c + i - 1)
                        for ct in range(NT):
                            m = tensor.matmul(out=ps_p[i % 2][:, 0:WA],
                                              lhsT=wk_sb[:, ct * C + i * 128:ct * C + (i + 1) * 128],
                                              rhs=xq[:, ct * WA:(ct + 1) * WA],
                                              start=(ct == 0), stop=(ct == NT - 1))
                        m.then_inc(mmk, 1)
                    tensor.wait_ge(xvr, c + 1)
                    for i in range(NT):
                        if i < 2:
                            tensor.wait_ge(ek, NT * c + 7 + i)
                        else:
                            tensor.wait_ge(gv, NT * c + i - 1)
                        for ct in range(NT):
                            m = tensor.matmul(out=ps_p[i % 2][:, 0:WA],
                                              lhsT=wv_sb[:, ct * C + i * 128:ct * C + (i + 1) * 128],
                                              rhs=xq[:, ct * WA:(ct + 1) * WA],
                                              start=(ct == 0), stop=(ct == NT - 1))
                        m.then_inc(mmv, 1)
                    tensor.wait_ge(xrr, c + 1)
                    for i in range(NT):
                        if i < 2:
                            tensor.wait_ge(gv, NT * c + 7 + i)
                        else:
                            tensor.wait_ge(rk, NT * c + i - 1)
                        for ct in range(NT):
                            m = tensor.matmul(out=ps_p[i % 2][:, 0:WA],
                                              lhsT=wr_sb[:, ct * C + i * 128:ct * C + (i + 1) * 128],
                                              rhs=xq[:, ct * WA:(ct + 1) * WA],
                                              start=(ct == 0), stop=(ct == NT - 1))
                        m.then_inc(mmr, 1)
                    tensor.wait_ge(rv, c + 1)
                    for j in range(NT):
                        g = NT * c + j
                        if g >= 2:
                            tensor.wait_ge(fin, g - 1)
                        tensor.wait_ge(wolp[g % 2], 16 * (g // 2 + 1))
                        for dt in range(NT):
                            m = tensor.matmul(out=ps_o[j % 2][:, 0:WA],
                                              lhsT=wost[:, ((g % 2) * NT + dt) * 128:((g % 2) * NT + dt + 1) * 128],
                                              rhs=rwb[:, dt * WA:(dt + 1) * WA],
                                              start=(dt == 0), stop=(dt == NT - 1))
                        m.then_inc(mmo, 1)

            @blk.gpsimd
            def _(gpsimd):
                pass



FH = 4096
NFH = FH // 128

def p_rwkv_ffn(nc, semf, srcy, srcz, dst, prm, wvf_scr, tag):
    es = ExitStack()
    sem = lambda n: semf(n)
    with es:
        sb = lambda n, s, d: es.enter_context(nc.sbuf_tensor(tag + n, s, d))
        zt = sb("zt", [128, 2 * NT * WR], F32)
        z1t = sb("z1t", [128, NT * WR], F32)
        xb = sb("xb", [128, 2 * NT * WR], BF16)
        xq = sb("xq", [128, 2 * NT * WR], BF16)
        wmx = sb("wmx", [128, NT * (WR + 1)], BF16)
        kfb = sb("kfb", [128, NFH * WR], BF16)
        rlb = sb("rlb", [128, 2 * WR], BF16)
        sr2 = sb("sr2", [128, NT * WR], BF16)
        mu = sb("mu", [128, WR], F32)
        t1 = sb("t1", [128, WR], F32)
        t2 = sb("t2", [128, WR], F32)
        t3 = sb("t3", [128, WR], F32)
        ss = sb("ss", [128, WR], F32)
        epsv = sb("epsv", [128, 1], F32)
        eps2v = sb("eps2v", [128, 1], F32)
        ones = sb("ones", [128, 128], BF16)
        prmt = sb("prmt", [128, 4 * NT], F32)
        omk = sb("omk", [128, 2 * NT], F32)
        wstg = sb("wstg", [128, 2 * FH], F32)
        wvq = sb("wvq", [128, 2 * C], BF16)
        wkf_sb = sb("wkf_sb", [128, NT * FH], BF16)
        wrf_sb = sb("wrf_sb", [128, NT * C], BF16)
        wvstr = sb("wvstr", [128, 2 * NFH * 128], BF16)
        ps_sum = es.enter_context(nc.psum_tensor(tag + "pss", [128, WR], F32))
        ps_sq = es.enter_context(nc.psum_tensor(tag + "psq", [128, WR], F32))
        ps_p0 = es.enter_context(nc.psum_tensor(tag + "pp0", [128, WR], F32))
        ps_p1 = es.enter_context(nc.psum_tensor(tag + "pp1", [128, WR], F32))
        ps_o0 = es.enter_context(nc.psum_tensor(tag + "po0", [128, WR], F32))
        ps_o1 = es.enter_context(nc.psum_tensor(tag + "po1", [128, WR], F32))
        ps_p = (ps_p0, ps_p1)
        ps_o = (ps_o0, ps_o1)

        ldp = (sem("ld0"), sem("ld1")); ldz = sem("ldz")
        prs = sem("prs"); wcv = sem("wcv")
        wldp = (sem("wld0"), sem("wld1"))
        wvsp = (sem("wvs0"), sem("wvs1"))
        wvlp = (sem("wvl0"), sem("wvl1"))
        vprep = sem("vprep")
        cast = sem("cast"); sq = sem("sq"); mmst = sem("mmst")
        sA = sem("sA"); vA = sem("vA"); sB = sem("sB"); vB = sem("vB"); sC = sem("sC")
        xkr = sem("xkr"); xrr = sem("xrr")
        mmk2 = sem("mmk2"); mmr2 = sem("mmr2"); mmkv = sem("mmkv")
        rh = sem("rh"); rk = sem("rk"); kfv = sem("kfv")
        fin = sem("fin"); st = sem("st")

        def BLKF(c, i):
            off = (c % 2) * NT * WR
            return slice(off + i * WR, off + (i + 1) * WR)

        with nc.Block(tag) as blk:
            @blk.sync
            def _(sync):
                for j, p in enumerate([prm["ln2g"], prm["ln2b"], prm["fmixk"], prm["fmixr"]]):
                    sync.dma_start(out=prmt[:, j * NT:(j + 1) * NT], in_=p).then_inc(prs, 16)
                for kk in range(8):
                    if kk >= 2:
                        sync.wait_ge(wcv, kk - 1)
                    sync.dma_start(out=wstg[:, (kk % 2) * FH:(kk % 2 + 1) * FH],
                                   in_=prm["wkfT"][kk * 128:(kk + 1) * 128, :]).then_inc(wldp[kk % 2], 16)
                for j in range(NFH):
                    kk = 8 + j
                    sync.wait_ge(wcv, kk - 1)
                    sync.dma_start(out=wstg[:, (kk % 2) * FH:(kk % 2) * FH + C],
                                   in_=prm["wvfT"][j * 128:(j + 1) * 128, :]).then_inc(wldp[kk % 2], 16)
                    sync.wait_ge(wcv, kk + 1)
                    sync.dma_start(out=wvf_scr[j * 128:(j + 1) * 128, :],
                                   in_=wvq[:, (j % 2) * C:(j % 2 + 1) * C]).then_inc(wvsp[j % 2], 16)
                for j in range(NT):
                    kk = 40 + j
                    sync.wait_ge(wcv, kk - 1)
                    sync.dma_start(out=wstg[:, (kk % 2) * FH:(kk % 2) * FH + C],
                                   in_=prm["wrfT"][j * 128:(j + 1) * 128, :]).then_inc(wldp[kk % 2], 16)
                srcy3 = srcy.rearrange("(i p) t -> p i t", p=128)
                srcz3 = srcz.rearrange("(i p) t -> p i t", p=128)
                dst3 = dst.rearrange("(i p) t -> p i t", p=128)
                z1t3 = z1t[:].rearrange("p (i t) -> p i t", i=NT)
                wvf3 = wvf_scr.rearrange("(kh p) c -> p kh c", p=128)
                def emit_load(cc):
                    if cc >= 2:
                        sync.wait_ge(st, 16 * (cc - 1))
                    sync.dma_start(out=zt[:, (cc % 2) * NT * WR:(cc % 2 + 1) * NT * WR].rearrange(
                                       "p (i t) -> p i t", i=NT),
                                   in_=srcy3[:, :, cc * WR:(cc + 1) * WR]).then_inc(ldp[cc % 2], 16)

                emit_load(0)
                if NCHR > 1:
                    emit_load(1)
                for c in range(NCHR):
                    if c > 0:
                        sync.wait_ge(st, 16 * c)
                    sync.dma_start(out=z1t3, in_=srcz3[:, :, c * WR:(c + 1) * WR]).then_inc(ldz, 16)
                    for ct in range(NT):
                        g = c * NT + ct
                        if g >= 2:
                            sync.wait_ge(mmkv, g - 1)
                        if g == 0:
                            sync.wait_ge(wvsp[0], 16 * (NFH // 2))
                            sync.wait_ge(wvsp[1], 16 * (NFH // 2))
                        sync.dma_start(
                            out=wvstr[:, (g % 2) * NFH * 128:((g % 2) + 1) * NFH * 128].rearrange(
                                "p (kh c) -> p kh c", kh=NFH),
                            in_=wvf3[:, :, ct * 128:(ct + 1) * 128]).then_inc(wvlp[g % 2], 16)
                    sync.wait_ge(fin, NT * (c + 1))
                    sync.dma_start(out=dst3[:, :, c * WR:(c + 1) * WR], in_=z1t3).then_inc(st, 16)
                    if c + 2 < NCHR:
                        emit_load(c + 2)
                sync.wait_ge(st, 16 * NCHR)

            @blk.scalar
            def _(scalar):
                for kk in range(8):
                    scalar.wait_ge(wldp[kk % 2], 16 * (kk // 2 + 1))
                    scalar.copy(out=wkf_sb[:, kk * FH:(kk + 1) * FH],
                                in_=wstg[:, (kk % 2) * FH:(kk % 2 + 1) * FH]).then_inc(wcv, 1)
                for j in range(NFH):
                    kk = 8 + j
                    scalar.wait_ge(wldp[kk % 2], 16 * (kk // 2 + 1))
                    if j >= 2:
                        scalar.wait_ge(wvsp[j % 2], 16 * (j // 2))
                    scalar.copy(out=wvq[:, (j % 2) * C:(j % 2 + 1) * C],
                                in_=wstg[:, (kk % 2) * FH:(kk % 2) * FH + C]).then_inc(wcv, 1)
                for j in range(NT):
                    kk = 40 + j
                    scalar.wait_ge(wldp[kk % 2], 16 * (kk // 2 + 1))
                    scalar.copy(out=wrf_sb[:, j * C:(j + 1) * C],
                                in_=wstg[:, (kk % 2) * FH:(kk % 2) * FH + C]).then_inc(wcv, 1)
                for c in range(NCHR):
                    scalar.wait_ge(ldp[c % 2], 16 * (c // 2 + 1))
                    if c >= 2:
                        scalar.wait_ge(mmst, c - 1)
                    for i in range(NT):
                        ins = scalar.copy(out=xb[:, BLKF(c, i)], in_=zt[:, BLKF(c, i)])
                    ins.then_inc(cast, 1)
                    if c >= 2:
                        scalar.wait_ge(mmr2, NT * (c - 1))
                    for i in range(NT):
                        ins = scalar.activation(out=xq[:, BLKF(c, i)],
                                                in_=xb[:, BLKF(c, i)], func=AF.Square)
                    ins.then_inc(sq, 1)
                    scalar.wait_ge(mmst, c + 1)
                    scalar.activation(out=mu[:], in_=ps_sum[:, 0:WR], func=AF.Copy, scale=1.0 / C)
                    scalar.activation(out=t1[:], in_=ps_sq[:, 0:WR], func=AF.Copy,
                                      scale=1.0 / C).then_inc(sA, 1)
                    scalar.wait_ge(vA, c + 1)
                    scalar.activation(out=t2[:], in_=t1[:], func=AF.Ln, bias=eps2v[:, 0:1],
                                      scale=float(1.0 + EPS))
                    scalar.activation(out=ss[:], in_=t2[:], func=AF.Exp, scale=-0.5).then_inc(sC, 1)
                    if c > 0:
                        scalar.wait_ge(mmkv, NT * c)
                    for h in range(NFH):
                        scalar.wait_ge(mmk2, NFH * c + h + 1)
                        scalar.activation(out=rlb[:, (h % 2) * WR:(h % 2 + 1) * WR],
                                          in_=ps_p[h % 2][:, 0:WR], func=AF.Relu).then_inc(rh, 1)
                        scalar.activation(out=kfb[:, h * WR:(h + 1) * WR],
                                          in_=rlb[:, (h % 2) * WR:(h % 2 + 1) * WR],
                                          func=AF.Square).then_inc(kfv, 1)
                    for i in range(NT):
                        scalar.wait_ge(mmr2, NT * c + i + 1)
                        scalar.activation(out=sr2[:, i * WR:(i + 1) * WR],
                                          in_=ps_p[i % 2][:, 0:WR], func=AF.Sigmoid).then_inc(rk, 1)

            @blk.vector
            def _(vector):
                mixbuf = xq
                vector.memset(ones[:], 1.0)
                vector.memset(epsv[:], float(EPS))
                vector.memset(eps2v[:], float(EPS * EPS))
                for i in range(NT):
                    vector.memset(wmx[:, i * (WR + 1):i * (WR + 1) + 1], 0.0)
                vector.wait_ge(prs, 64)
                for j in range(2):
                    vector.tensor_scalar(out=omk[:, j * NT:(j + 1) * NT],
                                         in0=prmt[:, (2 + j) * NT:(3 + j) * NT],
                                         scalar1=-1.0, scalar2=1.0, op0=ALU.mult,
                                         op1=ALU.add).then_inc(vprep, 1)
                gcol = prmt[:, 0:NT]
                bcol = prmt[:, NT:2 * NT]
                def emit_prologue(cc):
                    vector.wait_ge(sA, cc + 1)
                    vector.tensor_tensor(out=t2[:], in0=mu[:], in1=mu[:], op=ALU.mult)
                    vector.tensor_tensor(out=t1[:], in0=t1[:], in1=t2[:],
                                         op=ALU.subtract).then_inc(vA, 1)
                    vector.wait_ge(sC, cc + 1)
                    vector.wait_ge(mmst, cc + 1)
                    for i in range(NT):
                        vector.tensor_tensor(out=t2[:], in0=zt[:, BLKF(cc, i)], in1=mu[:],
                                             op=ALU.subtract)
                        vector.tensor_tensor(out=t2[:], in0=t2[:], in1=ss[:], op=ALU.mult)
                        vector.tensor_scalar(out=wmx[:, i * (WR + 1) + 1:(i + 1) * (WR + 1)],
                                             in0=t2[:], scalar1=gcol[:, i:i + 1],
                                             scalar2=bcol[:, i:i + 1], op0=ALU.mult, op1=ALU.add)
                    for i in range(NT):
                        wcur = wmx[:, i * (WR + 1) + 1:(i + 1) * (WR + 1)]
                        wsh = wmx[:, i * (WR + 1):(i + 1) * (WR + 1) - 1]
                        vector.tensor_scalar(out=mixbuf[:, BLKF(cc, i)], in0=wsh,
                                             scalar1=omk[:, i:i + 1], scalar2=None, op0=ALU.mult)
                        ins = vector.scalar_tensor_tensor(out=mixbuf[:, BLKF(cc, i)],
                                                          in0=wcur, scalar=prmt[:, 2 * NT + i:2 * NT + i + 1],
                                                          in1=mixbuf[:, BLKF(cc, i)],
                                                          op0=ALU.mult, op1=ALU.add)
                    ins.then_inc(xkr, 1)

                emit_prologue(0)
                for c in range(NCHR):
                    vector.wait_ge(mmk2, NFH * (c + 1))
                    for i in range(NT):
                        wcur = wmx[:, i * (WR + 1) + 1:(i + 1) * (WR + 1)]
                        wsh = wmx[:, i * (WR + 1):(i + 1) * (WR + 1) - 1]
                        vector.tensor_scalar(out=mixbuf[:, BLKF(c, i)], in0=wsh,
                                             scalar1=omk[:, NT + i:NT + i + 1], scalar2=None, op0=ALU.mult)
                        ins = vector.scalar_tensor_tensor(out=mixbuf[:, BLKF(c, i)],
                                                          in0=wcur, scalar=prmt[:, 3 * NT + i:3 * NT + i + 1],
                                                          in1=mixbuf[:, BLKF(c, i)],
                                                          op0=ALU.mult, op1=ALU.add)
                    ins.then_inc(xrr, 1)
                    for i in range(NT):
                        vector.tensor_copy(out=wmx[:, i * (WR + 1):i * (WR + 1) + 1],
                                           in_=wmx[:, (i + 1) * (WR + 1) - 1:(i + 1) * (WR + 1)])
                    if c + 1 < NCHR:
                        emit_prologue(c + 1)
                    vector.wait_ge(ldz, 16 * (c + 1))
                    for ct in range(NT):
                        vector.wait_ge(mmkv, NT * c + ct + 1)
                        vector.wait_ge(rk, NT * c + ct + 1)
                        sl = slice(ct * WR, (ct + 1) * WR)
                        vector.tensor_tensor(out=t2[:], in0=ps_o[ct % 2][:, 0:WR], in1=sr2[:, sl],
                                             op=ALU.mult)
                        vector.tensor_tensor(out=z1t[:, sl], in0=z1t[:, sl], in1=zt[:, BLKF(c, ct)],
                                             op=ALU.add)
                        vector.tensor_tensor(out=z1t[:, sl], in0=z1t[:, sl], in1=t2[:],
                                             op=ALU.add).then_inc(fin, 1)

            @blk.tensor
            def _(tensor):
                tensor.wait_ge(wcv, 48)
                tensor.wait_ge(vprep, 2)

                def emit_stats(cc):
                    tensor.wait_ge(cast, cc + 1)
                    tensor.wait_ge(sq, cc + 1)
                    if cc > 0:
                        tensor.wait_ge(sA, cc)
                    for i in range(NT):
                        tensor.matmul(out=ps_sum[:, 0:WR], lhsT=ones[:, 0:128], rhs=xb[:, BLKF(cc, i)],
                                      start=(i == 0), stop=(i == NT - 1))
                    for i in range(NT):
                        m = tensor.matmul(out=ps_sq[:, 0:WR], lhsT=ones[:, 0:128], rhs=xq[:, BLKF(cc, i)],
                                          start=(i == 0), stop=(i == NT - 1))
                    m.then_inc(mmst, 1)

                emit_stats(0)
                for c in range(NCHR):
                    tensor.wait_ge(xkr, c + 1)
                    for h in range(NFH):
                        if h >= 2:
                            tensor.wait_ge(rh, NFH * c + h - 1)
                        elif c > 0:
                            tensor.wait_ge(rk, NT * c + h - 1)
                        for ct in range(NT):
                            m = tensor.matmul(out=ps_p[h % 2][:, 0:WR],
                                              lhsT=wkf_sb[:, ct * FH + h * 128:ct * FH + (h + 1) * 128],
                                              rhs=xq[:, BLKF(c, ct)],
                                              start=(ct == 0), stop=(ct == NT - 1))
                        m.then_inc(mmk2, 1)
                    tensor.wait_ge(xrr, c + 1)
                    for i in range(NT):
                        if i < 2:
                            tensor.wait_ge(rh, NFH * c + 31 + i)
                        else:
                            tensor.wait_ge(rk, NT * c + i - 1)
                        for ct in range(NT):
                            m = tensor.matmul(out=ps_p[i % 2][:, 0:WR],
                                              lhsT=wrf_sb[:, ct * C + i * 128:ct * C + (i + 1) * 128],
                                              rhs=xq[:, BLKF(c, ct)],
                                              start=(ct == 0), stop=(ct == NT - 1))
                        m.then_inc(mmr2, 1)
                    if c + 1 < NCHR:
                        emit_stats(c + 1)
                    tensor.wait_ge(kfv, NFH * (c + 1))
                    for ct in range(NT):
                        g = c * NT + ct
                        if g >= 2:
                            tensor.wait_ge(fin, g - 1)
                        tensor.wait_ge(wvlp[g % 2], 16 * (g // 2 + 1))
                        for kh in range(NFH):
                            m = tensor.matmul(out=ps_o[ct % 2][:, 0:WR],
                                              lhsT=wvstr[:, ((g % 2) * NFH + kh) * 128:((g % 2) * NFH + kh + 1) * 128],
                                              rhs=kfb[:, kh * WR:(kh + 1) * WR],
                                              start=(kh == 0), stop=(kh == NFH - 1))
                        m.then_inc(mmkv, 1)

            @blk.gpsimd
            def _(gpsimd):
                pass



def p_l1attn(nc, semf, src, dst, prm, kscr, vscr, tag):
    NP = 8

    def G(c):
        return 4 * c * (c + 1)

    es = ExitStack()
    sem = lambda n: semf(n)
    with es:
        sb = lambda n, s, d: es.enter_context(nc.sbuf_tensor(tag + n, s, d))
        zt = sb("zt", [128, NT * W], F32)
        xb = sb("xb", [128, NT * W], BF16)
        xq = sb("xq", [128, NT * W], BF16)
        xnb = xq
        mu = sb("mu", [128, W], F32)
        t1 = sb("t1", [128, W], F32)
        t2 = sb("t2", [128, W], F32)
        t3 = sb("t3", [128, W], F32)
        epsv = sb("epsv", [128, 1], F32)
        ones = sb("ones", [128, 128], BF16)
        onesf = sb("onesf", [128, 128], F32)
        qstg = sb("qstg", [128, 2 * W], BF16)
        kstg = sb("kstg", [128, 2 * W], BF16)
        rtp = sb("rtp", [128, W], F32)
        qrope = sb("qrope", [128, NT * W], BF16)
        krope = sb("krope", [128, NT * W], BF16)
        rcosb = sb("rcosb", [128, W], F32)
        rsinb = sb("rsinb", [128, W], F32)
        bvrow = sb("bvrow", [1, C], F32)
        bvbc = sb("bvbc", [128, C], F32)
        vtmp = sb("vtmp", [128, 2 * W], BF16)
        vchunk = sb("vchunk", [128, 4 * 1040], BF16)
        kstr = sb("kstr", [128, 2 * W], BF16)
        vstr = sb("vstr", [128, 2 * 4 * 130], BF16)
        ebuf = sb("ebuf", [128, 16 * W], BF16)
        dbuf = sb("dbuf", [128, 2 * W], F32)
        btmp = sb("btmp", [128, 2 * W], BF16)
        ctxb = sb("ctxb", [128, NT * W], BF16)
        wtmp = sb("wtmp", [128, 2 * W], F32)
        prmt = sb("prmt", [128, 3 * NT], F32)
        cmk = sb("cmk", [128, 4 * W], BF16)
        sS = sb("sS", [128, 128], BF16)
        wstg = sb("wstg", [128, 2 * C], F32)
        wq_sb = sb("wq_sb", [128, NT * C], BF16)
        wk_sb = sb("wk_sb", [128, NT * C], BF16)
        wv_sb = sb("wv_sb", [128, NT * C], BF16)
        wo_sb = sb("wo_sb", [128, NT * C], BF16)
        ps_sum = es.enter_context(nc.psum_tensor(tag + "pss", [128, W], F32))
        ps_sq = es.enter_context(nc.psum_tensor(tag + "psq", [128, W], F32))
        ps_a0 = es.enter_context(nc.psum_tensor(tag + "pa0", [128, W], F32))
        ps_a1 = es.enter_context(nc.psum_tensor(tag + "pa1", [128, W], F32))
        ps_b0 = es.enter_context(nc.psum_tensor(tag + "pb0", [128, W], F32))
        ps_b1 = es.enter_context(nc.psum_tensor(tag + "pb1", [128, W], F32))
        ps_ca = es.enter_context(nc.psum_tensor(tag + "pca", [128, W], F32))
        ps_cb = es.enter_context(nc.psum_tensor(tag + "pcb", [128, W], F32))
        ps_a = (ps_a0, ps_a1)
        ps_b = (ps_b0, ps_b1)
        ps_v = (ps_ca, ps_cb)

        ld = sem("ld"); prs = sem("prs"); wcv = sem("wcv")
        wldp = (sem("wld0"), sem("wld1"))
        cast = sem("cast"); sq = sem("sq"); mmst = sem("mmst")
        sA = sem("sA"); vA = sem("vA"); sB = sem("sB")
        xn_s = sem("xn")
        mmq = sem("mmq"); mmk = sem("mmk"); mmv = sem("mmv")
        msw = sem("msw")
        qev = sem("qev"); kev = sem("kev"); vev = sem("vev")
        qrp = sem("qrp"); krp = sem("krp"); vcp = sem("vcp")
        kst = sem("kst"); vst = sem("vst"); rtl = sem("rtl")
        kslp = (sem("ksl0"), sem("ksl1"))
        vslp = (sem("vsl0"), sem("vsl1"))
        exv = sem("exv"); dmk = sem("dmk"); pvc = sem("pvc"); scg = sem("scg")
        cte = sem("cte"); rcv = sem("rcv"); bcv = sem("bcv"); ctn = sem("ctn")
        cbdp = (sem("cbd0"), sem("cbd1"))
        mwo = sem("mwo"); woev = sem("woev"); bvv = sem("bvv"); bve = sem("bve")
        fin = sem("fin"); st = sem("st")

        w_aps = [prm["qwT"], prm["kwT"], prm["vwT"], prm["owT"]]
        w_sbs = [wq_sb, wk_sb, wv_sb, wo_sb]

        with nc.Block(tag) as blk:
            @blk.sync
            def _(sync):
                for j, p in enumerate([prm["bq"], prm["bk"], prm["bo"]]):
                    sync.dma_start(out=prmt[:, j * NT:(j + 1) * NT], in_=p).then_inc(prs, 16)
                sync.dma_start(out=bvrow[:], in_=prm["bv_row"]).then_inc(prs, 16)
                sync.dma_start(out=cmk[:], in_=prm["cmask"]).then_inc(prs, 16)
                for kk in range(33):
                    if kk >= 2:
                        sync.wait_ge(wcv, kk - 1)
                    src_ap = (w_aps[kk // 8][(kk % 8) * 128:(kk % 8 + 1) * 128, :]
                              if kk < 32 else prm["ropeS"])
                    sync.dma_start(out=wstg[:, (kk % 2) * C:(kk % 2) * C + src_ap.shape[1]],
                                   in_=src_ap).then_inc(wldp[kk % 2], 16)
                src3 = src.rearrange("(i p) t -> p i t", p=128)
                dst3 = dst.rearrange("(i p) t -> p i t", p=128)
                zt3 = zt[:].rearrange("p (i t) -> p i t", i=NT)
                kscr3 = kscr.rearrange("(i p) t -> p i t", p=128)
                krope3 = krope[:].rearrange("p (i t) -> p i t", i=NT)
                vchunk3 = vchunk[:].rearrange("p (tt c) -> p tt c", tt=4)
                for c in range(NCH):
                    if c > 0:
                        sync.wait_ge(st, 16 * c)
                    sync.dma_start(out=zt3, in_=src3[:, :, c * W:(c + 1) * W]).then_inc(ld, 16)
                    if c > 0:
                        sync.wait_ge(krp, NT * c)
                    sync.dma_start(out=rcosb[:], in_=prm["rcos"][:, c * W:(c + 1) * W]).then_inc(rtl, 16)
                    sync.dma_start(out=rsinb[:], in_=prm["rsin"][:, c * W:(c + 1) * W]).then_inc(rtl, 16)
                    sync.wait_ge(krp, NT * (c + 1))
                    sync.dma_start(out=kscr3[:, :, c * W:(c + 1) * W], in_=krope3).then_inc(kst, 16)
                    sync.wait_ge(vcp, c + 1)
                    sync.dma_start(out=vscr[c * W:(c + 1) * W, :].rearrange("(tt p) c -> p tt c", p=128),
                                   in_=vchunk3).then_inc(vst, 16)
                    for j in range(NP):
                        for kc in range(c + 1):
                            g = G(c) + j * (c + 1) + kc
                            sync.wait_ge(kst, 16 * (kc + 1))
                            sync.wait_ge(vst, 16 * (kc + 1))
                            if g >= 2:
                                sync.wait_ge(scg, 8 * (g - 1))
                            sync.dma_start(out=kstr[:, (g % 2) * W:(g % 2 + 1) * W],
                                           in_=kscr[j * 128:(j + 1) * 128, kc * W:(kc + 1) * W]
                                           ).then_inc(kslp[g % 2], 16)
                            if g >= 2:
                                sync.wait_ge(pvc, g - 1)
                            sync.dma_start(
                                out=vstr[:, (g % 2) * 4 * 130:((g % 2) + 1) * 4 * 130].rearrange(
                                    "p (tt c) -> p tt c", tt=4),
                                in_=vscr[kc * W:(kc + 1) * W, j * 130:(j + 1) * 130].rearrange(
                                    "(tt p) c -> p tt c", p=128)).then_inc(vslp[g % 2], 16)
                        sync.wait_ge(cte, 4 * (NP * c + j) + 4)
                        sync.dma_start(out=ctxb[64:128, j * W:(j + 1) * W],
                                       in_=btmp[0:64, (j % 2) * W:(j % 2 + 1) * W]
                                       ).then_inc(cbdp[j % 2], 16)
                    sync.wait_ge(fin, NT * (c + 1))
                    sync.dma_start(out=dst3[:, :, c * W:(c + 1) * W], in_=zt3).then_inc(st, 16)
                sync.wait_ge(st, 16 * NCH)
                sync.wait_ge(kst, 16 * NCH)
                sync.wait_ge(vst, 16 * NCH)

            @blk.scalar
            def _(scalar):
                for kk in range(33):
                    scalar.wait_ge(wldp[kk % 2], 16 * (kk // 2 + 1))
                    out_sb = (w_sbs[kk // 8][:, (kk % 8) * C:(kk % 8 + 1) * C] if kk < 32
                              else sS[:])
                    ncol = C if kk < 32 else 128
                    scalar.copy(out=out_sb,
                                in_=wstg[:, (kk % 2) * C:(kk % 2) * C + ncol]).then_inc(wcv, 1)
                for c in range(NCH):
                    scalar.wait_ge(ld, 16 * (c + 1))
                    for i in range(NT):
                        ins = scalar.copy(out=xb[:, i * W:(i + 1) * W], in_=zt[:, i * W:(i + 1) * W])
                    ins.then_inc(cast, 1)
                    scalar.wait_ge(mmst, c + 1)
                    scalar.activation(out=mu[:], in_=ps_sum[:, 0:W], func=AF.Copy, scale=1.0 / C)
                    scalar.activation(out=t1[:], in_=ps_sq[:, 0:W], func=AF.Copy,
                                      scale=1.0 / C).then_inc(sA, 1)
                    scalar.wait_ge(vA, c + 1)
                    scalar.activation(out=t2[:], in_=t1[:], func=AF.Ln, bias=epsv[:, 0:1])
                    scalar.activation(out=t3[:], in_=t2[:], func=AF.Exp, scale=-0.5).then_inc(sB, 1)
                    for arr, mm_s, ev_s, rp_s, stg, pb in ((0, mmq, qev, qrp, qstg, 0),
                                                           (1, mmk, kev, krp, kstg, NT)):
                        for dt in range(NT):
                            scalar.wait_ge(mm_s, NT * c + dt + 1)
                            if dt >= 2:
                                scalar.wait_ge(rp_s, NT * c + dt - 1)
                            scalar.activation(out=stg[:, (dt % 2) * W:(dt % 2 + 1) * W],
                                              in_=(ps_a if arr == 0 else ps_b)[dt % 2][:, 0:W],
                                              func=AF.Identity,
                                              bias=prmt[:, pb + dt:pb + dt + 1]).then_inc(ev_s, 1)
                    for j in range(NP):
                        for kc in range(c + 1):
                            g = G(c) + j * (c + 1) + kc
                            if g >= 2:
                                scalar.wait_ge(pvc, g - 1)
                            for sub in range(4):
                                q0 = sub * 128 if kc == c else 0
                                for hh, psx in ((0, ps_a), (1, ps_b)):
                                    slot = (g % 2) * 8 + 2 * sub + hh
                                    scalar.wait_ge(scg, 8 * g + 2 * sub + hh + 1)
                                    scalar.activation(out=ebuf[:, slot * W + q0:(slot + 1) * W],
                                                      in_=psx[sub % 2][:, q0:W], func=AF.Exp,
                                                      scale=0.125).then_inc(exv, 1)
                        P4 = 4 * (NP * c + j)
                        gl = G(c) + j * (c + 1) + c
                        scalar.wait_ge(pvc, gl + 1)
                        if NP * c + j >= 2:
                            scalar.wait_ge(cbdp[j % 2], 16 * ((NP * c + j) // 2))
                        scalar.copy(out=dbuf[64:65, 0:W], in_=ps_ca[64:65, 0:W]).then_inc(cte, 1)
                        scalar.copy(out=dbuf[64:65, W:2 * W], in_=ps_cb[64:65, 0:W]).then_inc(cte, 1)
                        scalar.copy(out=ctxb[0:64, j * W:(j + 1) * W],
                                    in_=ps_ca[0:64, 0:W]).then_inc(cte, 1)
                        scalar.copy(out=btmp[0:64, (j % 2) * W:(j % 2 + 1) * W],
                                    in_=ps_cb[0:64, 0:W]).then_inc(cte, 1)
                    for jo in range(NT):
                        scalar.wait_ge(mwo, NT * c + jo + 1)
                        if NT * c + jo >= 2:
                            scalar.wait_ge(fin, NT * c + jo - 1)
                        scalar.activation(out=wtmp[:, (jo % 2) * W:(jo % 2 + 1) * W],
                                          in_=ps_a[jo % 2][:, 0:W], func=AF.Identity,
                                          bias=prmt[:, 2 * NT + jo:2 * NT + jo + 1]).then_inc(woev, 1)

            @blk.vector
            def _(vector):
                vector.memset(ones[:], 1.0)
                vector.memset(onesf[:], 1.0)
                vector.memset(epsv[:], float(EPS))
                vector.memset(vchunk[:], 1.0)
                vector.memset(ebuf[:], 0.0)
                for half in range(2):
                    vector.wait_ge(bvv, half + 1)
                    vector.tensor_copy(out=bvbc[:, half * W:(half + 1) * W],
                                       in_=ps_a[half][:, 0:W]).then_inc(bve, 1)
                for c in range(NCH):
                    vector.wait_ge(cast, c + 1)
                    for i in range(NT):
                        ins = vector.tensor_tensor(out=xq[:, i * W:(i + 1) * W],
                                                   in0=xb[:, i * W:(i + 1) * W],
                                                   in1=xb[:, i * W:(i + 1) * W], op=ALU.mult)
                    ins.then_inc(sq, 1)
                    vector.wait_ge(sA, c + 1)
                    vector.tensor_tensor(out=t2[:], in0=mu[:], in1=mu[:], op=ALU.mult)
                    vector.tensor_tensor(out=t1[:], in0=t1[:], in1=t2[:],
                                         op=ALU.subtract).then_inc(vA, 1)
                    vector.wait_ge(sB, c + 1)
                    vector.wait_ge(mmst, c + 1)
                    for i in range(NT):
                        sl = slice(i * W, (i + 1) * W)
                        vector.tensor_tensor(out=t2[:], in0=zt[:, sl], in1=mu[:], op=ALU.subtract)
                        ins = vector.tensor_tensor(out=xnb[:, sl], in0=t2[:], in1=t3[:], op=ALU.mult)
                    ins.then_inc(xn_s, 1)
                    vector.wait_ge(rtl, 32 * (c + 1))
                    if c > 0:
                        vector.wait_ge(kst, 16 * c)
                    for arr, stg, outb, rp_s in ((0, qstg, qrope, qrp), (1, kstg, krope, krp)):
                        for dt in range(NT):
                            vector.wait_ge(msw, 16 * c + arr * NT + dt + 1)
                            ssl = slice((dt % 2) * W, (dt % 2 + 1) * W)
                            bsl = slice(dt * W, (dt + 1) * W)
                            psx = (ps_a if arr == 0 else ps_b)[dt % 2]
                            vector.tensor_tensor(out=rtp[:], in0=stg[:, ssl], in1=rcosb[:], op=ALU.mult)
                            vector.tensor_tensor(out=t2[:], in0=psx[:, 0:W], in1=rsinb[:], op=ALU.mult)
                            vector.tensor_tensor(out=outb[:, bsl], in0=rtp[:], in1=t2[:],
                                                 op=ALU.add).then_inc(rp_s, 1)
                    if c > 0:
                        vector.wait_ge(vst, 16 * c)
                    for idx in range(NT):
                        tt, eh = idx // 2, idx % 2
                        vector.wait_ge(mmv, NT * c + idx + 1)
                        vector.tensor_tensor(out=vtmp[:, (idx % 2) * W:(idx % 2 + 1) * W],
                                             in0=ps_v[idx % 2][:, 0:W],
                                             in1=bvbc[:, eh * W:(eh + 1) * W], op=ALU.add).then_inc(vev, 1)
                        for h8 in range(8):
                            h = eh * 8 + h8
                            ins = vector.tensor_copy(
                                out=vchunk[:, tt * 1040 + h * 65:tt * 1040 + h * 65 + 64],
                                in_=vtmp[:, (idx % 2) * W + h8 * 64:(idx % 2) * W + (h8 + 1) * 64])
                    ins.then_inc(vcp, 1)
                    for j in range(NP):
                        gd = G(c) + j * (c + 1) + c
                        for sub in range(4):
                            for hh in range(2):
                                slot = (gd % 2) * 8 + 2 * sub + hh
                                vector.wait_ge(exv, 8 * gd + 2 * sub + hh + 1)
                                vector.tensor_tensor(out=ebuf[:, slot * W:(slot + 1) * W],
                                                     in0=ebuf[:, slot * W:(slot + 1) * W],
                                                     in1=cmk[:, sub * W:(sub + 1) * W],
                                                     op=ALU.mult).then_inc(dmk, 1)
                        P4 = 4 * (NP * c + j)
                        vector.wait_ge(cte, P4 + 1)
                        vector.reciprocal(out=dbuf[64:65, 0:W], in_=dbuf[64:65, 0:W]).then_inc(rcv, 1)
                        vector.wait_ge(cte, P4 + 2)
                        vector.reciprocal(out=dbuf[64:65, W:2 * W], in_=dbuf[64:65, W:2 * W]).then_inc(rcv, 1)
                        vector.wait_ge(bcv, 2 * (NP * c + j) + 1)
                        vector.wait_ge(cte, P4 + 3)
                        vector.tensor_tensor(out=ctxb[0:64, j * W:(j + 1) * W],
                                             in0=ctxb[0:64, j * W:(j + 1) * W],
                                             in1=ps_sum[0:64, 0:W], op=ALU.mult).then_inc(ctn, 1)
                        vector.wait_ge(bcv, 2 * (NP * c + j) + 2)
                        vector.wait_ge(cbdp[j % 2], 16 * ((NP * c + j) // 2 + 1))
                        vector.tensor_tensor(out=ctxb[64:128, j * W:(j + 1) * W],
                                             in0=ctxb[64:128, j * W:(j + 1) * W],
                                             in1=ps_sq[64:128, 0:W], op=ALU.mult).then_inc(ctn, 1)
                    for jo in range(NT):
                        vector.wait_ge(woev, NT * c + jo + 1)
                        vector.tensor_tensor(out=zt[:, jo * W:(jo + 1) * W],
                                             in0=zt[:, jo * W:(jo + 1) * W],
                                             in1=wtmp[:, (jo % 2) * W:(jo % 2 + 1) * W],
                                             op=ALU.add).then_inc(fin, 1)

            @blk.tensor
            def _(tensor):
                tensor.wait_ge(wcv, 33)
                tensor.wait_ge(prs, 80)
                for half in range(2):
                    tensor.matmul(out=ps_a[half][:, 0:W], lhsT=onesf[0:1, 0:128],
                                  rhs=bvrow[0:1, half * W:(half + 1) * W],
                                  start=True, stop=True).then_inc(bvv, 1)
                for c in range(NCH):
                    tensor.wait_ge(cast, c + 1)
                    tensor.wait_ge(sq, c + 1)
                    if c > 0:
                        tensor.wait_ge(sA, c)
                        tensor.wait_ge(ctn, 16 * c)
                    m = ln_stats_chunk_tensor(tensor, ones, xb, xq, ps_sum, ps_sq)
                    m.then_inc(mmst, 1)
                    tensor.wait_ge(xn_s, c + 1)
                    for arr, w_sb, mm_s, ev_s, rp_s, psp in ((0, wq_sb, mmq, qev, qrp, ps_a),
                                                             (1, wk_sb, mmk, kev, krp, ps_b)):
                        for dt in range(NT):
                            if dt >= 2:
                                tensor.wait_ge(rp_s, NT * c + dt - 1)
                            elif c > 0:
                                if arr == 0:
                                    tensor.wait_ge(woev, NT * c)
                                else:
                                    tensor.wait_ge(exv, 8 * G(c))
                            elif c == 0 and arr == 0:
                                tensor.wait_ge(bve, 2)
                            for ct in range(NT):
                                m = tensor.matmul(out=psp[dt % 2][:, 0:W],
                                                  lhsT=w_sb[:, ct * C + dt * 128:ct * C + (dt + 1) * 128],
                                                  rhs=xnb[:, ct * W:(ct + 1) * W],
                                                  start=(ct == 0), stop=(ct == NT - 1))
                            m.then_inc(mm_s, 1)
                            stg = qstg if arr == 0 else kstg
                            tensor.wait_ge(ev_s, NT * c + dt + 1)
                            tensor.matmul(out=psp[dt % 2][:, 0:W], lhsT=sS[:],
                                          rhs=stg[:, (dt % 2) * W:(dt % 2 + 1) * W],
                                          start=True, stop=True).then_inc(msw, 1)
                    for idx in range(NT):
                        tt, eh = idx // 2, idx % 2
                        if idx >= 2:
                            tensor.wait_ge(vev, NT * c + idx - 1)
                        elif c > 0:
                            tensor.wait_ge(cte, 32 * c)
                        psx = ps_v[idx % 2]
                        for ct in range(NT):
                            m = tensor.matmul(out=psx[:, 0:W],
                                              lhsT=xnb[:, ct * W + tt * 128:ct * W + (tt + 1) * 128],
                                              rhs=wv_sb[:, ct * C + eh * W:ct * C + (eh + 1) * W],
                                              start=(ct == 0), stop=(ct == NT - 1))
                        m.then_inc(mmv, 1)
                    for j in range(NP):
                        for kc in range(c + 1):
                            g = G(c) + j * (c + 1) + kc
                            tensor.wait_ge(kslp[g % 2], 16 * (g // 2 + 1))
                            tensor.wait_ge(vslp[g % 2], 16 * (g // 2 + 1))
                            for sub in range(4):
                                q0 = sub * 128 if kc == c else 0
                                for hh, psx, tp in ((0, ps_a, 0), (1, ps_b, 64)):
                                    war = 8 * g + 2 * sub + hh - 3
                                    if kc == 0 and j == 0 and sub < 2:
                                        tensor.wait_ge(qrp, NT * (c + 1))
                                        tensor.wait_ge(krp, NT * (c + 1))
                                    elif war >= 1:
                                        tensor.wait_ge(exv, war)
                                    tensor.matmul(
                                        out=psx[sub % 2][:, q0:W],
                                        lhsT=kstr[tp:tp + 64, (g % 2) * W + sub * 128:(g % 2) * W + (sub + 1) * 128],
                                        rhs=qrope[tp:tp + 64, j * W + q0:(j + 1) * W],
                                        start=True, stop=True,
                                        tile_position=(tp, 0)).then_inc(scg, 1)
                            for sub in range(4):
                                for hh, psc in ((0, ps_ca), (1, ps_cb)):
                                    slot = (g % 2) * 8 + 2 * sub + hh
                                    if kc == c:
                                        tensor.wait_ge(dmk, 8 * (NP * c + j) + 2 * sub + hh + 1)
                                    else:
                                        tensor.wait_ge(exv, 8 * g + 2 * sub + hh + 1)
                                    if kc == 0 and sub == 0 and hh == 0:
                                        if j > 0:
                                            tensor.wait_ge(cte, 4 * (NP * c + j))
                                        else:
                                            tensor.wait_ge(vev, NT * (c + 1))
                                    m = tensor.matmul(
                                        out=psc[0:65, 0:W],
                                        lhsT=vstr[:, ((g % 2) * 4 + sub) * 130 + hh * 65:
                                                  ((g % 2) * 4 + sub) * 130 + (hh + 1) * 65],
                                        rhs=ebuf[:, slot * W:(slot + 1) * W],
                                        start=(kc == 0 and sub == 0), stop=(kc == c and sub == 3))
                                    if sub == 3 and hh == 1:
                                        m.then_inc(pvc, 1)
                        for hh in range(2):
                            tensor.wait_ge(rcv, 2 * (NP * c + j) + hh + 1)
                            if j > 0:
                                tensor.wait_ge(ctn, 16 * c + 2 * j)
                            elif c > 0:
                                tensor.wait_ge(ctn, 16 * c)
                            if hh == 0:
                                tensor.matmul(out=ps_sum[0:64, 0:W], lhsT=onesf[64:65, 0:64],
                                              rhs=dbuf[64:65, 0:W], start=True, stop=True,
                                              tile_position=(64, 0)).then_inc(bcv, 1)
                            else:
                                tensor.matmul(out=ps_sq[64:128, 0:W], lhsT=onesf[64:65, 0:64],
                                              rhs=dbuf[64:65, W:2 * W], start=True, stop=True,
                                              tile_position=(64, 64)).then_inc(bcv, 1)
                    tensor.wait_ge(ctn, 16 * (c + 1))
                    tensor.wait_ge(exv, 8 * G(c + 1))
                    for jo in range(NT):
                        if NT * c + jo >= 2:
                            tensor.wait_ge(woev, NT * c + jo - 1)
                        for et in range(NT):
                            m = tensor.matmul(out=ps_a[jo % 2][:, 0:W],
                                              lhsT=wo_sb[:, et * C + jo * 128:et * C + (jo + 1) * 128],
                                              rhs=ctxb[:, et * W:(et + 1) * W],
                                              start=(et == 0), stop=(et == NT - 1))
                        m.then_inc(mwo, 1)

            @blk.gpsimd
            def _(gpsimd):
                pass



def _vec_tile(v):
    v = np.asarray(v).reshape(-1)
    return np.ascontiguousarray(v.reshape(-1, 128).T.astype(np.float32))


def _build_program(phases):
    nc = bass.Bass("TRN2", target_bir_lowering=False, detect_race_conditions=False)
    spec = {
        "x": ([C, T], F32, "ExternalInput"),
        "x2": ([C, T], F32, "ExternalInput"),
        "out": ([C, T], F32, "ExternalOutput"),
        "X": ([C, T], F32, "Internal"),
        "Y": ([C, T], F32, "Internal"),
        "l0f_g": ([128, NT], F32, "ExternalInput"),
        "l0f_b": ([128, NT], F32, "ExternalInput"),
        "wgT": ([C, HID], I32, "ExternalInput"),
        "wuT": ([C, HID], I32, "ExternalInput"),
        "wdT": ([HID, C], I32, "ExternalInput"),
        "WDscr": ([HID, C], BF16, "Internal"),
        "WUscr": ([C, HID], BF16, "Internal"),
        "WOscr": ([C, C], BF16, "Internal"),
        "wkT": ([C, C], F32, "ExternalInput"),
        "wvT": ([C, C], F32, "ExternalInput"),
        "wrT": ([C, C], F32, "ExternalInput"),
        "woT": ([C, C], F32, "ExternalInput"),
        "wkfT": ([C, FH], F32, "ExternalInput"),
        "wvfT": ([FH, C], F32, "ExternalInput"),
        "wrfT": ([C, C], F32, "ExternalInput"),
        "WVFscr": ([FH, C], BF16, "Internal"),
        "qwT": ([C, C], F32, "ExternalInput"),
        "kwT": ([C, C], F32, "ExternalInput"),
        "vwT": ([C, C], F32, "ExternalInput"),
        "owT": ([C, C], F32, "ExternalInput"),
        "rcos": ([128, T], F32, "ExternalInput"),
        "rsin": ([128, T], F32, "ExternalInput"),
        "ropeS": ([128, 128], F32, "ExternalInput"),
        "bv_row": ([1, C], F32, "ExternalInput"),
        "cmask": ([128, 4 * W], BF16, "ExternalInput"),
        "Kscr": ([C, T], BF16, "Internal"),
        "Vscr": ([T, 1040], BF16, "Internal"),
    }
    for n in ["ln1g", "ln1b", "mixk", "mixv", "mixr", "td", "tf",
              "ln2g", "ln2b", "fmixk", "fmixr", "bq", "bk", "bv", "bo"]:
        spec[n] = ([128, NT], F32, "ExternalInput")
    ap = {}

    def get(k):
        if k not in ap:
            sh, dt_, kind = spec[k]
            ap[k] = nc.dram_tensor(k, sh, dt_, kind=kind).ap()
        return ap[k]

    es_prog = ExitStack()
    with es_prog:
        NPOOL = 48
        pool = [es_prog.enter_context(nc.semaphore(f"pool{i}")) for i in range(NPOOL)]

        def make_semf():
            idx = [0]

            def semf(name):
                h = pool[idx[0]]
                idx[0] += 1
                assert idx[0] <= NPOOL, "semaphore pool exhausted"
                return h
            return semf

        def fence(k):
            with nc.Block(f"fence{k}") as fb:
                @fb.sync
                def _(sync):
                    for sh in pool:
                        sync.sem_clear(sh)

                @fb.scalar
                def _(scalar):
                    pass

                @fb.vector
                def _(vector):
                    pass

                @fb.tensor
                def _(tensor):
                    pass

                @fb.gpsimd
                def _(gpsimd):
                    pass

        for pi, ph in enumerate(phases):
            name = ph[0]
            tagp = f"ph{pi}_"
            fence(pi)
            semf = make_semf()
            if name == "l0ffn":
                p_l0ffn(nc, semf, get(ph[1]), get(ph[2]), get("l0f_g"), get("l0f_b"), tag=tagp)
            elif name == "l1ffn":
                p_l1ffn(nc, semf, get(ph[1]), get(ph[2]), get("wgT"), get("wuT"), get("wdT"),
                        get("WDscr"), get("WUscr"), tag=tagp)
            elif name == "rwkv_att":
                prm = {n: get(n) for n in ["wkT", "wvT", "wrT", "woT", "ln1g", "ln1b",
                                           "mixk", "mixv", "mixr", "td", "tf"]}
                p_rwkv_att(nc, semf, get(ph[1]), get(ph[2]), prm, get("WOscr"), tag=tagp)
            elif name == "l1attn":
                prm = {n: get(n) for n in ["qwT", "kwT", "vwT", "owT", "bq", "bk", "bo",
                                           "bv_row", "rcos", "rsin", "ropeS", "cmask"]}
                p_l1attn(nc, semf, get(ph[1]), get(ph[2]), prm, get("Kscr"), get("Vscr"),
                         tag=tagp)
            elif name == "rwkv_ffn":
                prm = {n: get(n) for n in ["wkfT", "wvfT", "wrfT", "ln2g", "ln2b",
                                           "fmixk", "fmixr"]}
                p_rwkv_ffn(nc, semf, get(ph[1]), get(ph[2]), get(ph[3]), prm,
                           get("WVFscr"), tag=tagp)
    return nc, set(ap.keys())


def _host_inputs(inputs):
    assert inputs["L0_ffn_attn_mask"].shape == (T, T)
    assert inputs["L0_ffn_Wg"].shape == (C, 16)
    assert inputs["L0_ffn_Wq"].shape[0] * inputs["L0_ffn_Wq"].shape[2] == 16 * 64
    assert inputs["L0_ffn_bq"].shape == (16, 64)
    assert inputs["L0_ffn_Wkv"].shape == (16, C, 128)
    assert inputs["L0_ffn_bkv"].shape == (16, 128)
    assert inputs["L0_ffn_Wo"].shape == (16, 64, C)
    d = {}
    d["l0f_g"] = _vec_tile(inputs["L0_ffn_ln_g"])
    d["l0f_b"] = _vec_tile(inputs["L0_ffn_ln_b"])
    d["wkT"] = np.ascontiguousarray(np.asarray(inputs["L0_attn_Wk"], dtype=np.float32).T)
    d["wvT"] = np.ascontiguousarray(np.asarray(inputs["L0_attn_Wv"], dtype=np.float32).T)
    d["wrT"] = np.ascontiguousarray(np.asarray(inputs["L0_attn_Wr"], dtype=np.float32).T)
    d["woT"] = np.ascontiguousarray(np.asarray(inputs["L0_attn_Wo"], dtype=np.float32).T)
    d["ln1g"] = _vec_tile(inputs["L0_attn_ln1_g"])
    d["ln1b"] = _vec_tile(inputs["L0_attn_ln1_b"])
    d["mixk"] = _vec_tile(inputs["L0_attn_att_mix_k"])
    d["mixv"] = _vec_tile(inputs["L0_attn_att_mix_v"])
    d["mixr"] = _vec_tile(inputs["L0_attn_att_mix_r"])
    d["td"] = _vec_tile(inputs["L0_attn_time_decay"])
    d["tf"] = _vec_tile(inputs["L0_attn_time_first"])
    d["wkfT"] = np.ascontiguousarray(np.asarray(inputs["L0_attn_Wk_ffn"], dtype=np.float32).T)
    d["wvfT"] = np.ascontiguousarray(np.asarray(inputs["L0_attn_Wv_ffn"], dtype=np.float32).T)
    d["wrfT"] = np.ascontiguousarray(np.asarray(inputs["L0_attn_Wr_ffn"], dtype=np.float32).T)
    d["ln2g"] = _vec_tile(inputs["L0_attn_ln2_g"])
    d["ln2b"] = _vec_tile(inputs["L0_attn_ln2_b"])
    d["fmixk"] = _vec_tile(inputs["L0_attn_ffn_mix_k"])
    d["fmixr"] = _vec_tile(inputs["L0_attn_ffn_mix_r"])
    d["qwT"] = np.ascontiguousarray(np.asarray(inputs["L1_attn_Wq"], dtype=np.float32).T)
    d["kwT"] = np.ascontiguousarray(np.asarray(inputs["L1_attn_Wk"], dtype=np.float32).T)
    d["vwT"] = np.ascontiguousarray(np.asarray(inputs["L1_attn_Wv"], dtype=np.float32).T)
    d["owT"] = np.ascontiguousarray(np.asarray(inputs["L1_attn_Wo"], dtype=np.float32).T)
    d["bq"] = _vec_tile(inputs["L1_attn_bq"])
    d["bk"] = _vec_tile(inputs["L1_attn_bk"])
    d["bo"] = _vec_tile(inputs["L1_attn_bo"])
    d["bv_row"] = np.ascontiguousarray(
        np.asarray(inputs["L1_attn_bv"], dtype=np.float32).reshape(1, C))
    rr = np.arange(128)
    ii = (rr % 64) // 2
    tpos = np.arange(T, dtype=np.float64)
    ang = tpos[None, :] / (10000.0 ** (2.0 * ii[:, None] / 64.0))
    d["rcos"] = np.cos(ang).astype(np.float32)
    d["rsin"] = np.sin(ang).astype(np.float32)
    S = np.zeros((128, 128), np.float32)
    for blk in (0, 64):
        for i in range(32):
            S[blk + 2 * i + 1, blk + 2 * i] = -1.0
            S[blk + 2 * i, blk + 2 * i + 1] = 1.0
    d["ropeS"] = S
    cm = np.zeros((128, 4 * W), np.float32)
    kk_ = np.arange(128)
    qq_ = np.arange(W)
    for o in range(4):
        cm[:, o * W:(o + 1) * W] = (128 * o + kk_[:, None]) <= qq_[None, :]
    import ml_dtypes
    d["cmask"] = cm.astype(ml_dtypes.bfloat16)
    d["wgT"] = np.ascontiguousarray(np.asarray(inputs["L1_ffn_w_gate"]).T)
    d["wuT"] = np.ascontiguousarray(np.asarray(inputs["L1_ffn_w_up"]).T)
    d["wdT"] = np.ascontiguousarray(np.asarray(inputs["L1_ffn_w_down"]).T)
    return d


def kernel(**inputs):
    x = np.asarray(inputs["L0_attn_x"])
    shared = _host_inputs(inputs)
    phases = [
        ("l0ffn", "x", "X"),
        ("rwkv_att", "X", "Y"),
        ("rwkv_ffn", "Y", "X", "X"),
        ("l0ffn", "X", "X"),
        ("l1ffn", "X", "X"),
        ("l1attn", "X", "X"),
        ("l1ffn", "X", "out"),
    ]
    nc, used = _build_program(phases)
    in_maps = []
    for core in range(8):
        m = {k: v for k, v in shared.items() if k in used}
        m["x"] = np.ascontiguousarray(x[core].T)
        in_maps.append(m)
    res = run_bass_kernel_spmd(nc, in_maps, core_ids=list(range(8)))
    out = np.stack([np.asarray(r["out"]).T for r in res.results])
    return out.astype(np.float32)
